# version 17
# baseline (speedup 1.0000x reference)
import os
import numpy as np
from contextlib import ExitStack, contextmanager

import concourse.bass as bass
import concourse.mybir as mybir
from concourse.bass_utils import run_bass_kernel_spmd

F32 = mybir.dt.float32
BF16 = mybir.dt.bfloat16
I32 = mybir.dt.int32
AF = mybir.ActivationFunctionType
ALU = mybir.AluOpType
AX = mybir.AxisListType.X

T, D, DC, DR, NH = 8192, 2048, 512, 64, 16
FI = 8192
NTOK = 1024
NCORES = 8
NTT = 8
NGRP = 16
SLOT_KTS = (32, 64)
WIN = 8
NGB = 8
SCALE = float(1.0 / np.sqrt(128 + 64))
EPS = 1e-5
INV_D = float(1.0 / D)

_DEBUG = bool(int(os.environ.get("MK_DEBUG", "0")))
_PHASES = os.environ.get("MK_PHASES", "all")


class _DummyInstr:
    def then_inc(self, *a, **k):
        return self


class _DummyEng:
    def __getattr__(self, name):
        if name == "wait_ge":
            return lambda *a, **k: None
        return lambda *a, **k: _DummyInstr()


class Env:

    def __init__(self):
        self.marks = {}
        self.cnt = {}
        self.passno = 1
        self.cond_deltas = {}
        self.sem_reg = {}

    def begin_pass(self, passno):
        self.passno = passno
        if passno == 1:
            self.snapshot = dict(self.cnt)
        else:
            self.final = dict(self.cnt)
            self.cnt = dict(self.snapshot)

    def end_phase_check(self):
        if self.passno == 2:
            assert self.cnt == self.final, (
                f"pass mismatch: {self.cnt} vs {self.final}")

    def inc(self, instr, sem, k=1, mark=None):
        instr.then_inc(sem.h, k) if self.passno == 2 else None
        self.cnt[sem.name] = self.cnt.get(sem.name, 0) + k
        if mark is not None:
            key = (sem.name, mark)
            if self.passno == 1:
                self.marks[key] = self.cnt[sem.name]
            else:
                assert self.marks[key] == self.cnt[sem.name], key
        return self.cnt[sem.name]

    def mark_here(self, sem, mark):
        key = (sem.name, mark)
        cur = self.cnt.get(sem.name, 0)
        if self.passno == 1:
            self.marks[key] = cur
        else:
            assert self.marks[key] == cur, key

    def wait(self, eng, sem, mark):
        if self.passno == 2:
            eng.wait_ge(sem.h, self.marks[(sem.name, mark)])

    def wait_val(self, eng, sem, val):
        if self.passno == 2:
            eng.wait_ge(sem.h, val)

    @contextmanager
    def cond(self, eng, key, flag_ap):
        if self.passno == 1:
            snap = dict(self.cnt)
            yield
            self.cond_deltas[key] = {
                k: v - snap.get(k, 0)
                for k, v in self.cnt.items() if v != snap.get(k, 0)}
        else:
            rv = eng.value_load(flag_ap)
            with eng.If(rv):
                yield
            with eng.Else():
                for sname, d in sorted(self.cond_deltas[key].items()):
                    eng.sem_inc(self.sem_reg[sname].h, d)


class Sem:
    def __init__(self, es, nc, name):
        self.h = es.enter_context(nc.semaphore(name))
        self.name = name


def build_program():
    nc = bass.Bass("TRN2", target_bir_lowering=False,
                   detect_race_conditions=False)
    env = Env()

    def inp(name, shape, dt=F32):
        return nc.dram_tensor(name, shape, dt, kind="ExternalInput").ap()

    x_own = inp("x_own", [NTOK, D])
    x_full = inp("x_full", [T, D])
    qpos = inp("qpos", [1, NTOK])
    cosq = inp("cosq", [32, NTOK])
    sinq = inp("sinq", [32, NTOK])
    cosk = inp("cosk", [32, T])
    sink = inp("sink", [32, T])
    iota = inp("iota", [128, 1])
    identb = inp("identb", [128, 128], BF16)
    win_en = inp("win_en", [1, 16], I32)
    Wkv = inp("Wkv", [D, DC])
    Wkup = inp("Wkup", [DC, D])
    Wvup = inp("Wvup", [DC, D])
    Wq = inp("Wq", [D, D])
    Wqr = inp("Wqr", [D, DR])
    Wkr = inp("Wkr", [DC, DR])
    Wo = inp("Wo", [D, D])
    wg_p = inp("wg_p", [D, FI])
    wu_p = inp("wu_p", [D, FI])
    wd_p = inp("wd_p", [FI, D])
    out_own = nc.dram_tensor("out_own", [NTOK, D], F32, kind="ExternalOutput").ap()

    skind = "ExternalOutput" if _DEBUG else "Internal"
    kT_s = nc.dram_tensor("kT_s", [D, T], BF16, kind=skind).ap()
    v_s = nc.dram_tensor("v_s", [T, D], BF16, kind=skind).ap()
    krT_s = nc.dram_tensor("krT_s", [DR, T], BF16, kind=skind).ap()
    xnT_s = nc.dram_tensor("xnT_s", [128, 16 * NTOK], BF16, kind=skind).ap()
    qT_s = nc.dram_tensor("qT_s", [128, NH * NTOK], BF16, kind=skind).ap()
    avn_s = nc.dram_tensor("avn_s", [128, NH * NTOK], BF16, kind=skind).ap()
    rs_d = nc.dram_tensor("rs_d", [2, NH * 512], F32, kind="Internal").ap()
    av_s = nc.dram_tensor("av_s", [2 * 128, NH * 512], F32, kind="Internal").ap()

    es = ExitStack()
    sb = lambda name, shape, dt: es.enter_context(nc.sbuf_tensor(name, shape, dt))
    iota_t = sb("iota_t", [128, 1], F32)
    ident_t = sb("ident_t", [128, 128], BF16)
    win_fl = sb("win_fl", [1, 16], I32)
    ones_b = sb("ones_b", [128, 1], BF16)
    ones_f1 = sb("ones_f1", [1, 128], F32)
    epst = sb("epst", [128, 1], F32)
    esQ = ExitStack()
    sbQ = lambda name, shape, dt: esQ.enter_context(nc.sbuf_tensor(name, shape, dt))
    qpos_bc = sbQ("qpos_bc", [128, NTOK], F32)
    qrT_lo = sbQ("qrT_lo", [32, NTOK], BF16)
    qrT_hi = sbQ("qrT_hi", [32, NTOK], BF16)

    def S(name):
        s = Sem(es, nc, name)
        env.sem_reg[name] = s
        return s
    s_ld = S("s_ld"); s_w = S("s_w")
    s_xo = [S(f"s_xo{i}") for i in range(2)]
    s_qsp = [S(f"s_qsp{i}") for i in range(2)]
    s_wq = [S(f"s_wq{i}") for i in range(2)]
    s_xnt = S("s_xnt")
    s_xs = [S(f"s_xs{i}") for i in range(6)]
    s_cs = [S(f"s_cs{i}") for i in range(2)]
    s_spk = S("s_spk"); s_spr = S("s_spr"); s_spv = S("s_spv")
    s_qsl = S("s_qsl")
    s_krw = [S(f"s_krw{i}") for i in range(2)]
    s_kvh = [S(f"s_kvh{i}") for i in range(2)]
    s_avnsp = [S(f"s_avnsp{i}") for i in range(2)]
    s_wo = [S(f"s_wo{i}") for i in range(2)]
    s_a25 = [S(f"s_a25{i}") for i in range(2)]
    s_w3 = [S(f"s_w3{i}") for i in range(2)]
    s_wd = S("s_wd"); s_xnt3 = S("s_xnt3"); s_ms = S("s_ms"); s_rsp = S("s_rsp")
    s_pq = S("s_pq"); s_ptr = S("s_ptr"); s_pac = S("s_pac")
    s_ar = S("s_ar"); s_phw = S("s_phw"); s_pbc = S("s_pbc")
    s_aev = S("s_aev"); s_apr = S("s_apr"); s_asq = S("s_asq"); s_astd = S("s_astd")
    s_aear = S("s_aear"); s_asl = S("s_asl")
    s_v = S("s_v"); s_vt = S("s_vt"); s_va = S("s_va"); s_vp = S("s_vp")
    s_vear = S("s_vear"); s_vav = S("s_vav"); s_vrs = S("s_vrs"); s_vavn = S("s_vavn")
    s_vh = S("s_vh"); s_vhs = S("s_vhs"); s_vd = S("s_vd"); s_out = S("s_out")

    def fence(eng, instr):
        c = env.inc(instr, s_vt, 1)
        env.wait_val(eng, s_vt, c)

    def run_phase(name, bodies):
        env.begin_pass(1)
        for eng_name in ("sync", "gpsimd", "tensor", "scalar", "vector"):
            if eng_name in bodies:
                bodies[eng_name](env, _DummyEng())
        env.begin_pass(2)
        with nc.Block(name) as blk:
            for eng_name in ("sync", "gpsimd", "tensor", "scalar", "vector"):
                if eng_name in bodies:
                    getattr(blk, eng_name)(
                        lambda eng, f=bodies[eng_name]: f(env, eng))
        env.end_phase_check()

    esW = ExitStack()
    sbW = lambda name, shape, dt: esW.enter_context(nc.sbuf_tensor(name, shape, dt))
    wkv_b = sbW("wkv_b", [128, 16 * DC], BF16)
    wkup_b = sbW("wkup_b", [128, 4 * D], BF16)
    wvup_b = sbW("wvup_b", [128, 4 * D], BF16)
    wqr_b = sbW("wqr_b", [128, 16 * DR], BF16)
    wkr_b = sbW("wkr_b", [128, 4 * DR], BF16)

    es0 = ExitStack()
    sb0 = lambda name, shape, dt: es0.enter_context(nc.sbuf_tensor(name, shape, dt))
    wqh = [sb0(f"wqh{i}", [128, 16 * 128], BF16) for i in range(2)]
    xnT_own = sb0("xnT_own", [128, 16 * NTOK], BF16)
    qst = [sb0(f"qst{i}", [128, NTOK], BF16) for i in range(2)]
    qpos_t = sb0("qpos_t", [1, NTOK], F32)
    cq_t = sb0("cq_t", [32, NTOK], F32)
    sq_t = sb0("sq_t", [32, NTOK], F32)
    xo = [sb0(f"xo{i}", [128, D], F32) for i in range(2)]
    xn_st = [sb0(f"xn_st{i}", [128, D], BF16) for i in range(2)]
    sq_scr = sb0("sq_scr", [128, D], BF16)
    sums0 = [sb0(f"sums0{i}", [128, 1], F32) for i in range(2)]
    sqa0 = [sb0(f"sqa0{i}", [128, 1], F32) for i in range(2)]
    mt0 = sb0("mt0", [128, 1], F32)
    vt0 = sb0("vt0", [128, 1], F32)
    m20 = sb0("m20", [128, 1], F32)
    std0 = [sb0(f"std0{i}", [128, 1], F32) for i in range(2)]
    rstd0 = sb0("rstd0", [128, 1], F32)
    qr_lo = sb0("qr_lo", [32, NTOK], F32)
    qr_hi = sb0("qr_hi", [32, NTOK], F32)
    rt1 = sb0("rt1", [32, NTOK], F32)
    rt2 = sb0("rt2", [32, NTOK], F32)

    ps0 = ExitStack()
    pT = [ps0.enter_context(nc.psum_tensor(f"pT{i}", [128, 512], BF16)) for i in range(2)]
    pP = [ps0.enter_context(nc.psum_tensor(f"pP{i}", [128, 512], F32)) for i in range(2)]
    pQR = [ps0.enter_context(nc.psum_tensor(f"pQR{i}", [32, 512], F32)) for i in range(2)]

    def p0_sync(env, eng):
        env.inc(eng.dma_start(out=iota_t[:, :], in_=iota), s_ld, 16)
        env.inc(eng.dma_start(out=ident_t[:, :], in_=identb), s_ld, 16)
        env.inc(eng.dma_start(out=qpos_t[:, :], in_=qpos), s_ld, 16)
        env.inc(eng.dma_start(out=cq_t[:, :], in_=cosq), s_ld, 16)
        env.inc(eng.dma_start(out=sq_t[:, :], in_=sinq), s_ld, 16)
        env.inc(eng.dma_start(out=win_fl[:, :], in_=win_en), s_ld, 16, mark="consts")
        xov = x_own.rearrange("(n p) d -> p n d", p=128)
        for tt in range(NTT):
            if tt >= 2:
                env.wait(eng, s_v, ("p0xn", tt - 2))
                env.wait(eng, s_asq, ("p0sq", tt - 2))
            env.inc(eng.dma_start(out=xo[tt % 2][:, :], in_=xov[:, tt, :]),
                    s_xo[tt % 2], 16, mark=("p0x", tt))
        env.wait(eng, s_aev, ("p0ev", NTT - 1, 3))
        env.inc(eng.dma_start(out=xnT_s, in_=xnT_own[:, :]), s_xnt, 16, mark="xnT")
        for h in range(NH):
            env.wait(eng, s_apr, ("p0q", h, 1))
            env.inc(eng.dma_start(out=qT_s[:, h * NTOK:(h + 1) * NTOK],
                                  in_=qst[h % 2][:, :]),
                    s_qsp[h % 2], 16, mark=("p0qsp", h))

    def p0_gpsimd(env, eng):
        eng.memset(ones_b[:, :], 1.0)
        eng.memset(ones_f1[:, :], 1.0)
        env.inc(eng.memset(epst[:, :], EPS), s_ms, 1, mark="consts")
        env.inc(eng.dma_start(out=wkv_b[:, :].rearrange("p (a f) -> p a f", a=16),
                              in_=Wkv.rearrange("(a p) f -> p a f", p=128)), s_w, 16)
        env.inc(eng.dma_start(out=wkup_b[:, :].rearrange("p (a f) -> p a f", a=4),
                              in_=Wkup.rearrange("(a p) f -> p a f", p=128)), s_w, 16)
        env.inc(eng.dma_start(out=wvup_b[:, :].rearrange("p (a f) -> p a f", a=4),
                              in_=Wvup.rearrange("(a p) f -> p a f", p=128)), s_w, 16)
        env.inc(eng.dma_start(out=wqr_b[:, :].rearrange("p (a f) -> p a f", a=16),
                              in_=Wqr.rearrange("(a p) f -> p a f", p=128)), s_w, 16)
        env.inc(eng.dma_start(out=wkr_b[:, :].rearrange("p (a f) -> p a f", a=4),
                              in_=Wkr.rearrange("(a p) f -> p a f", p=128)),
                s_w, 16, mark="attnw")
        wqv = Wq.rearrange("(a p) f -> p a f", p=128)
        for h in range(NH):
            if h >= 2:
                env.wait(eng, s_pq, ("p0qm", h - 2, 1))
            env.inc(eng.dma_start(
                out=wqh[h % 2][:, :].rearrange("p (a f) -> p a f", a=16),
                in_=wqv[:, :, h * 128:(h + 1) * 128]),
                s_wq[h % 2], 16, mark=("p0wq", h))

    def p0_tensor(env, eng):
        env.wait(eng, s_ms, "consts")
        env.wait(eng, s_ld, "consts")
        for s in range(2):
            env.inc(eng.matmul(pP[s][:, :], ones_f1[:, :],
                               qpos_t[:, s * 512:(s + 1) * 512],
                               start=True, stop=True), s_pq, 1, mark=("p0bc", s))
        for tt in range(NTT):
            env.wait(eng, s_v, ("p0xn", tt))
            for j in range(4):
                gidx = tt * 4 + j
                if gidx >= 2:
                    pv = gidx - 2
                    env.wait(eng, s_aev, ("p0ev", pv // 4, pv % 4))
                ins = None
                for jj in range(4):
                    dt_i = j * 4 + jj
                    ins = eng.transpose(pT[gidx % 2][:, jj * 128:(jj + 1) * 128],
                                        xn_st[tt % 2][:, dt_i * 128:(dt_i + 1) * 128],
                                        ident_t[:, :])
                env.inc(ins, s_ptr, 1, mark=("p0tr", tt, j))
        env.wait(eng, s_va, ("p0bcv", 1))
        env.wait(eng, s_aev, ("p0ev", NTT - 1, 3))
        for h in range(NH):
            env.wait(eng, s_wq[h % 2], ("p0wq", h))
            for qh in range(2):
                g = h * 2 + qh
                if g >= 2:
                    pv = g - 2
                    env.wait(eng, s_apr, ("p0q", pv // 2, pv % 2))
                ins = None
                for dt_i in range(16):
                    ins = eng.matmul(
                        pP[g % 2][:, :],
                        wqh[h % 2][:, dt_i * 128:(dt_i + 1) * 128],
                        xnT_own[:, dt_i * NTOK + qh * 512: dt_i * NTOK + (qh + 1) * 512],
                        start=(dt_i == 0), stop=(dt_i == 15))
                env.inc(ins, s_pq, 1, mark=("p0qm", h, qh))
        for qh in range(2):
            for half in range(2):
                if qh == 1:
                    env.wait(eng, s_apr, ("p0qr", 0, half))
                ins = None
                for dt_i in range(16):
                    ins = eng.matmul(
                        pQR[half][:, :],
                        wqr_b[:, dt_i * DR + half * 32: dt_i * DR + half * 32 + 32],
                        xnT_own[:, dt_i * NTOK + qh * 512: dt_i * NTOK + (qh + 1) * 512],
                        start=(dt_i == 0), stop=(dt_i == 15))
                env.inc(ins, s_pq, 1, mark=("p0qrm", qh, half))

    def p0_scalar(env, eng):
        env.wait(eng, s_ms, "consts")
        for tt in range(NTT):
            env.wait(eng, s_xo[tt % 2], ("p0x", tt))
            if tt >= 2:
                env.wait(eng, s_va, ("p0var", tt - 2))
            env.inc(eng.activation(out=sq_scr[:, :], in_=xo[tt % 2][:, :],
                                   func=AF.Square, accum_out=sqa0[tt % 2][:, :]),
                    s_asq, 1, mark=("p0sq", tt))
            env.wait(eng, s_va, ("p0var", tt))
            if tt >= 2:
                env.wait(eng, s_vt, ("p0rec", tt - 2))
            env.inc(eng.activation(out=std0[tt % 2][:, :], in_=vt0[:, :],
                                   func=AF.Sqrt, bias=epst[:, :]),
                    s_astd, 1, mark=("p0std", tt))
            for j in range(4):
                env.wait(eng, s_ptr, ("p0tr", tt, j))
                dst = xnT_own[:, :].rearrange("p (a n t) -> p a n t", a=16, n=NTT)
                env.inc(eng.copy(out=dst[:, j * 4:(j + 1) * 4, tt, :],
                                 in_=pT[(tt * 4 + j) % 2][:, :].rearrange(
                                     "p (a t) -> p a t", a=4)),
                        s_aev, 1, mark=("p0ev", tt, j))
        for h in range(NH):
            for qh in range(2):
                env.wait(eng, s_pq, ("p0qm", h, qh))
                if h >= 2 and qh == 0:
                    env.wait(eng, s_qsp[h % 2], ("p0qsp", h - 2))
                env.inc(eng.activation(
                    out=qst[h % 2][:, qh * 512:(qh + 1) * 512],
                    in_=pP[(h * 2 + qh) % 2][:, :], func=AF.Copy, scale=SCALE),
                    s_apr, 1, mark=("p0q", h, qh))
        for qh in range(2):
            for half in range(2):
                env.wait(eng, s_pq, ("p0qrm", qh, half))
                dst = qr_lo if half == 0 else qr_hi
                env.inc(eng.activation(out=dst[:, qh * 512:(qh + 1) * 512],
                                       in_=pQR[half][:, :], func=AF.Copy, scale=SCALE),
                        s_apr, 1, mark=("p0qr", qh, half))

    def p0_vector(env, eng):
        for s in range(2):
            env.wait(eng, s_pq, ("p0bc", s))
            env.inc(eng.tensor_copy(out=qpos_bc[:, s * 512:(s + 1) * 512],
                                    in_=pP[s][:, :]),
                    s_va, 1, mark=("p0bcv", s))
        for tt in range(NTT):
            env.wait(eng, s_xo[tt % 2], ("p0x", tt))
            fence(eng, eng.reduce_sum(sums0[tt % 2][:, :], xo[tt % 2][:, :], axis=AX))
            fence(eng, eng.tensor_scalar(out=mt0[:, :], in0=sums0[tt % 2][:, :],
                                         scalar1=INV_D, scalar2=None, op0=ALU.mult))
            env.wait(eng, s_asq, ("p0sq", tt))
            fence(eng, eng.tensor_scalar(out=vt0[:, :], in0=sqa0[tt % 2][:, :],
                                         scalar1=INV_D, scalar2=None, op0=ALU.mult))
            fence(eng, eng.tensor_tensor(out=m20[:, :], in0=mt0[:, :], in1=mt0[:, :],
                                         op=ALU.mult))
            vins = eng.tensor_tensor(out=vt0[:, :], in0=vt0[:, :], in1=m20[:, :],
                                     op=ALU.subtract)
            c = env.inc(vins, s_va, 1, mark=("p0var", tt))
            env.wait_val(eng, s_va, c)
            env.wait(eng, s_astd, ("p0std", tt))
            rins = eng.reciprocal(out=rstd0[:, :], in_=std0[tt % 2][:, :])
            fence(eng, rins)
            env.mark_here(s_vt, ("p0rec", tt))
            if tt >= 2:
                env.wait(eng, s_ptr, ("p0tr", tt - 2, 3))
            env.inc(eng.tensor_scalar(out=xn_st[tt % 2][:, :], in0=xo[tt % 2][:, :],
                                      scalar1=mt0[:, :], scalar2=rstd0[:, :],
                                      op0=ALU.subtract, op1=ALU.mult),
                    s_v, 1, mark=("p0xn", tt))
        env.wait(eng, s_apr, ("p0qr", 1, 1))
        env.wait(eng, s_ld, "consts")
        eng.tensor_tensor(out=rt1[:, :], in0=qr_lo[:, :], in1=cq_t[:, :], op=ALU.mult)
        eng.tensor_tensor(out=rt2[:, :], in0=qr_hi[:, :], in1=sq_t[:, :], op=ALU.mult)
        eng.tensor_tensor(out=qrT_lo[:, :], in0=rt1[:, :], in1=rt2[:, :], op=ALU.subtract)
        eng.tensor_tensor(out=rt1[:, :], in0=qr_hi[:, :], in1=cq_t[:, :], op=ALU.mult)
        eng.tensor_tensor(out=rt2[:, :], in0=qr_lo[:, :], in1=sq_t[:, :], op=ALU.mult)
        env.inc(eng.tensor_tensor(out=qrT_hi[:, :], in0=rt1[:, :], in1=rt2[:, :],
                                  op=ALU.add), s_va, 1, mark="qrT")

    run_phase("p0", {"sync": p0_sync, "gpsimd": p0_gpsimd, "tensor": p0_tensor,
                     "scalar": p0_scalar, "vector": p0_vector})
    ps0.close()
    es0.close()

    es1 = ExitStack()
    sb1 = lambda name, shape, dt: es1.enter_context(nc.sbuf_tensor(name, shape, dt))
    xs = [sb1(f"xs{i}", [128, D], F32) for i in range(6)]
    xn1 = [sb1(f"xn1_{i}", [128, D], BF16) for i in range(6)]
    xnT_g = sb1("xnT_g", [128, 16 * 512], BF16)
    ckvT_g = sb1("ckvT_g", [128, 4 * 512], BF16)
    krr_lo = sb1("krr_lo", [32, 512], F32)
    krr_hi = sb1("krr_hi", [32, 512], F32)
    krb_lo = sb1("krb_lo", [32, 512], BF16)
    krb_hi = sb1("krb_hi", [32, 512], BF16)
    kT_st = sb1("kT_st", [128, 16 * 512], BF16)
    v_st = sb1("v_st", [128, 4 * D], BF16)
    sq1 = sb1("sq1", [128, D], BF16)
    sums1 = [sb1(f"sums1_{i}", [128, 1], F32) for i in range(2)]
    sqa1 = [sb1(f"sqa1_{i}", [128, 1], F32) for i in range(2)]
    m1 = sb1("m1", [128, 1], F32)
    v1 = sb1("v1", [128, 1], F32)
    m21 = sb1("m21", [128, 1], F32)
    std1 = [sb1(f"std1_{i}", [128, 1], F32) for i in range(2)]
    rstd1 = sb1("rstd1", [128, 1], F32)
    rr1 = sb1("rr1", [32, 512], F32)
    rr2 = sb1("rr2", [32, 512], F32)
    ckg = [sb1(f"ckg{i}", [32, 512], F32) for i in range(2)]
    skg = [sb1(f"skg{i}", [32, 512], F32) for i in range(2)]

    ps1 = ExitStack()
    pT1 = [ps1.enter_context(nc.psum_tensor(f"pT1_{i}", [128, 512], BF16)) for i in range(2)]
    pP1 = [ps1.enter_context(nc.psum_tensor(f"pP1_{i}", [128, 512], F32)) for i in range(2)]

    SESS = 38

    def p1_sync(env, eng):
        xfv = x_full.rearrange("(n p) d -> p n d", p=128)
        kTd = kT_s.rearrange("(h p) t -> p h t", p=128)
        vd = v_s.rearrange("(n p) d -> p n d", p=128)

        def loads(g):
            if g >= 2:
                env.wait(eng, s_va, ("p1rope", g - 2))
            env.inc(eng.dma_start(out=ckg[g % 2][:, :],
                                  in_=cosk[:, g * 512:(g + 1) * 512]), s_cs[g % 2], 16)
            env.inc(eng.dma_start(out=skg[g % 2][:, :],
                                  in_=sink[:, g * 512:(g + 1) * 512]),
                    s_cs[g % 2], 16, mark=("p1cs", g))
            for t4 in range(4):
                i = g * 4 + t4
                if i >= 6:
                    env.wait(eng, s_v, ("p1xn", i - 6))
                    env.wait(eng, s_asq, ("p1sq", i - 6))
                env.inc(eng.dma_start(out=xs[i % 6][:, :], in_=xfv[:, i, :]),
                        s_xs[i % 6], 16, mark=("p1x", i))

        def spills(g):
            env.wait(eng, s_apr, ("p1ev", g, 4 + 2 + 16 - 1))
            env.inc(eng.dma_start(
                out=kTd[:, :, g * 512:(g + 1) * 512],
                in_=kT_st[:, :].rearrange("p (h t) -> p h t", h=16)),
                s_spk, 16, mark=("p1spk", g))
            env.wait(eng, s_va, ("p1rope", g))
            env.inc(eng.dma_start(out=krT_s[0:32, g * 512:(g + 1) * 512],
                                  in_=krb_lo[:, :]), s_spr, 16)
            env.inc(eng.dma_start(out=krT_s[32:64, g * 512:(g + 1) * 512],
                                  in_=krb_hi[:, :]), s_spr, 16, mark=("p1spr", g))
            env.wait(eng, s_va, ("p1evv", g, SESS - 1))
            env.inc(eng.dma_start(
                out=vd[:, g * 4:(g + 1) * 4, :],
                in_=v_st[:, :].rearrange("p (n d) -> p n d", n=4)),
                s_spv, 16, mark=("p1spv", g))

        loads(0)
        loads(1)
        for g in range(NGRP):
            if g + 2 < NGRP:
                loads(g + 2)
            spills(g)

    def p1_tensor(env, eng):
        for g in range(NGRP):
            for t4 in range(4):
                i = g * 4 + t4
                env.wait(eng, s_v, ("p1xn", i))
                for j in range(4):
                    gi = i * 4 + j
                    if gi >= 2:
                        pv = gi - 2
                        env.wait(eng, s_aev, ("p1evT", pv // 4, pv % 4))
                    ins = None
                    for jj in range(4):
                        dt_i = j * 4 + jj
                        ins = eng.transpose(pT1[gi % 2][:, jj * 128:(jj + 1) * 128],
                                            xn1[i % 6][:, dt_i * 128:(dt_i + 1) * 128],
                                            ident_t[:, :])
                    env.inc(ins, s_ptr, 1, mark=("p1tr", i, j))
            env.wait(eng, s_aev, ("p1evT", g * 4 + 3, 3))
            if g == 0:
                env.wait(eng, s_w, "attnw")
            sess = [0]

            def psess(emitter, mark_extra=None):
                gs = g * SESS + sess[0]
                if gs >= 2:
                    pv = gs - 2
                    pg, psi = pv // SESS, pv % SESS
                    if psi >= 22:
                        env.wait(eng, s_va, ("p1evv", pg, psi))
                    else:
                        env.wait(eng, s_apr, ("p1ev", pg, psi))
                ins = emitter(pP1[gs % 2])
                env.inc(ins, s_pq, 1, mark=("p1mm", g, sess[0]))
                sess[0] += 1

            for dct in range(4):
                def emit(bank, dct=dct):
                    ins = None
                    for dt_i in range(16):
                        ins = eng.matmul(
                            bank[:, :],
                            wkv_b[:, dt_i * DC + dct * 128: dt_i * DC + (dct + 1) * 128],
                            xnT_g[:, dt_i * 512:(dt_i + 1) * 512],
                            start=(dt_i == 0), stop=(dt_i == 15))
                    return ins
                psess(emit)
            env.wait(eng, s_apr, ("p1ev", g, 3))

            for half in range(2):
                def emit_kr(bank, half=half):
                    ins = None
                    for dct in range(4):
                        ins = eng.matmul(
                            bank[0:32, :],
                            wkr_b[:, dct * DR + half * 32: dct * DR + half * 32 + 32],
                            ckvT_g[:, dct * 512:(dct + 1) * 512],
                            start=(dct == 0), stop=(dct == 3))
                    return ins
                psess(emit_kr)
            for h in range(NH):
                def emit_k(bank, h=h):
                    ins = None
                    for dct in range(4):
                        ins = eng.matmul(
                            bank[:, :],
                            wkup_b[:, dct * D + h * 128: dct * D + (h + 1) * 128],
                            ckvT_g[:, dct * 512:(dct + 1) * 512],
                            start=(dct == 0), stop=(dct == 3))
                    return ins
                psess(emit_k)
            for t4 in range(4):
                for vg in range(4):
                    def emit_v(bank, t4=t4, vg=vg):
                        ins = None
                        for dct in range(4):
                            ins = eng.matmul(
                                bank[:, :],
                                ckvT_g[:, dct * 512 + t4 * 128: dct * 512 + (t4 + 1) * 128],
                                wvup_b[:, dct * D + vg * 512: dct * D + (vg + 1) * 512],
                                start=(dct == 0), stop=(dct == 3))
                        return ins
                    psess(emit_v)

    def p1_scalar(env, eng):
        def sq_tiles(g):
            for t4 in range(4):
                i = g * 4 + t4
                env.wait(eng, s_xs[i % 6], ("p1x", i))
                if i >= 2:
                    env.wait(eng, s_va, ("p1var", i - 2))
                env.inc(eng.activation(out=sq1[:, :], in_=xs[i % 6][:, :],
                                       func=AF.Square, accum_out=sqa1[i % 2][:, :]),
                        s_asq, 1, mark=("p1sq", i))
                env.wait(eng, s_va, ("p1var", i))
                if i >= 2:
                    env.wait(eng, s_vt, ("p1rec", i - 2))
                env.inc(eng.activation(out=std1[i % 2][:, :], in_=v1[:, :],
                                       func=AF.Sqrt, bias=epst[:, :]),
                        s_astd, 1, mark=("p1std", i))

        def sq_tile(g, t4):
            i = g * 4 + t4
            env.wait(eng, s_xs[i % 6], ("p1x", i))
            if i >= 2:
                env.wait(eng, s_va, ("p1var", i - 2))
            env.inc(eng.activation(out=sq1[:, :], in_=xs[i % 6][:, :],
                                   func=AF.Square, accum_out=sqa1[i % 2][:, :]),
                    s_asq, 1, mark=("p1sq", i))
            env.wait(eng, s_va, ("p1var", i))
            if i >= 2:
                env.wait(eng, s_vt, ("p1rec", i - 2))
            env.inc(eng.activation(out=std1[i % 2][:, :], in_=v1[:, :],
                                   func=AF.Sqrt, bias=epst[:, :]),
                    s_astd, 1, mark=("p1std", i))

        def evacs_xnT_tile(g, t4):
            if True:
                i = g * 4 + t4
                for j in range(4):
                    gi = i * 4 + j
                    env.wait(eng, s_ptr, ("p1tr", i, j))
                    if g >= 1 and t4 == 0 and j == 0:
                        env.wait(eng, s_pq, ("p1mm", g - 1, SESS - 1))
                    dst = xnT_g[:, :].rearrange("p (a n t) -> p a n t", a=16, n=4)
                    env.inc(eng.copy(out=dst[:, j * 4:(j + 1) * 4, t4, :],
                                     in_=pT1[gi % 2][:, :].rearrange(
                                         "p (a t) -> p a t", a=4)),
                            s_aev, 1, mark=("p1evT", i, j))

        def pevacs(g):
            sess = [0]

            def pevac(dst_ap, part32=False):
                si = sess[0]
                env.wait(eng, s_pq, ("p1mm", g, si))
                bank = pP1[(g * SESS + si) % 2]
                srcb = bank[0:32, :] if part32 else bank[:, :]
                env.inc(eng.copy(out=dst_ap, in_=srcb), s_apr, 1, mark=("p1ev", g, si))
                sess[0] += 1

            if g >= 1:
                env.wait(eng, s_spk, ("p1spk", g - 1))
                env.wait(eng, s_spr, ("p1spr", g - 1))
            for dct in range(4):
                pevac(ckvT_g[:, dct * 512:(dct + 1) * 512])
            pevac(krr_lo[:, :], part32=True)
            pevac(krr_hi[:, :], part32=True)
            for h in range(NH):
                pevac(kT_st[:, h * 512:(h + 1) * 512])

        sq_tiles(0)
        for g in range(NGRP):
            for t4 in range(4):
                if g + 1 < NGRP:
                    sq_tile(g + 1, t4)
                evacs_xnT_tile(g, t4)
            pevacs(g)

    def p1_vector(env, eng):
        def ln_tiles(g):
            for t4 in range(4):
                i = g * 4 + t4
                env.wait(eng, s_xs[i % 6], ("p1x", i))
                fence(eng, eng.reduce_sum(sums1[i % 2][:, :], xs[i % 6][:, :], axis=AX))
                fence(eng, eng.tensor_scalar(out=m1[:, :], in0=sums1[i % 2][:, :],
                                             scalar1=INV_D, scalar2=None, op0=ALU.mult))
                env.wait(eng, s_asq, ("p1sq", i))
                fence(eng, eng.tensor_scalar(out=v1[:, :], in0=sqa1[i % 2][:, :],
                                             scalar1=INV_D, scalar2=None, op0=ALU.mult))
                fence(eng, eng.tensor_tensor(out=m21[:, :], in0=m1[:, :], in1=m1[:, :],
                                             op=ALU.mult))
                vins = eng.tensor_tensor(out=v1[:, :], in0=v1[:, :], in1=m21[:, :],
                                         op=ALU.subtract)
                c = env.inc(vins, s_va, 1, mark=("p1var", i))
                env.wait_val(eng, s_va, c)
                env.wait(eng, s_astd, ("p1std", i))
                rins = eng.reciprocal(out=rstd1[:, :], in_=std1[i % 2][:, :])
                fence(eng, rins)
                env.mark_here(s_vt, ("p1rec", i))
                if i >= 6:
                    env.wait(eng, s_ptr, ("p1tr", i - 6, 3))
                env.inc(eng.tensor_scalar(out=xn1[i % 6][:, :], in0=xs[i % 6][:, :],
                                          scalar1=m1[:, :], scalar2=rstd1[:, :],
                                          op0=ALU.subtract, op1=ALU.mult),
                        s_v, 1, mark=("p1xn", i))

        def tail(g):
            env.wait(eng, s_apr, ("p1ev", g, 5))
            env.wait(eng, s_cs[g % 2], ("p1cs", g))
            if g >= 1:
                env.wait(eng, s_spr, ("p1spr", g - 1))
            ck_g, sk_g = ckg[g % 2], skg[g % 2]
            eng.tensor_tensor(out=rr1[:, :], in0=krr_lo[:, :], in1=ck_g[:, :], op=ALU.mult)
            eng.tensor_tensor(out=rr2[:, :], in0=krr_hi[:, :], in1=sk_g[:, :], op=ALU.mult)
            eng.tensor_tensor(out=krb_lo[:, :], in0=rr1[:, :], in1=rr2[:, :], op=ALU.subtract)
            eng.tensor_tensor(out=rr1[:, :], in0=krr_hi[:, :], in1=ck_g[:, :], op=ALU.mult)
            eng.tensor_tensor(out=rr2[:, :], in0=krr_lo[:, :], in1=sk_g[:, :], op=ALU.mult)
            env.inc(eng.tensor_tensor(out=krb_hi[:, :], in0=rr1[:, :], in1=rr2[:, :],
                                      op=ALU.add), s_va, 1, mark=("p1rope", g))
            if g >= 1:
                env.wait(eng, s_spv, ("p1spv", g - 1))
            for vi in range(16):
                si = 22 + vi
                t4, vg = divmod(vi, 4)
                env.wait(eng, s_pq, ("p1mm", g, si))
                env.inc(eng.tensor_copy(
                    out=v_st[:, t4 * D + vg * 512: t4 * D + (vg + 1) * 512],
                    in_=pP1[(g * SESS + si) % 2][:, :]),
                    s_va, 1, mark=("p1evv", g, si))

        ln_tiles(0)
        for g in range(NGRP):
            if g + 1 < NGRP:
                ln_tiles(g + 1)
            tail(g)

    run_phase("p1", {"sync": p1_sync, "tensor": p1_tensor,
                     "scalar": p1_scalar, "vector": p1_vector})
    ps1.close()
    es1.close()
    esW.close()

    es2 = ExitStack()
    sb2 = lambda name, shape, dt: es2.enter_context(nc.sbuf_tensor(name, shape, dt))
    qT_sl = [sb2(f"qT_sl{i}", [128, NH * 512], BF16) for i in range(2)]
    ktw = [sb2(f"ktw{i}", [128, 8 * WIN * 128], BF16) for i in range(2)]
    vw = [sb2(f"vw{i}", [128, WIN * 8 * 128], BF16) for i in range(2)]
    krw_lo = [sb2(f"krw_lo{i}", [32, WIN * 128], BF16) for i in range(2)]
    krw_hi = [sb2(f"krw_hi{i}", [32, WIN * 128], BF16) for i in range(2)]
    av_sb = sb2("av_sb", [128, NH * 512], F32)
    rs_sb = sb2("rs_sb", [1, NH * 512], F32)
    rsp_t = sb2("rsp_t", [128, 64], F32)
    earR = [sb2("earR0", [128, 512], F32)]
    earM = sb2("earM", [128, WIN * 512], BF16)
    mask_t = sb2("mask_t", [128, 512], F32)
    p_buf = [sb2(f"p_buf{i}", [128, 1024], BF16) for i in range(4)]
    pr_buf = [sb2(f"pr_buf{i}", [128, 1024], BF16) for i in range(4)]


    ps2 = ExitStack()
    pac2 = [ps2.enter_context(nc.psum_tensor(f"pac2_{i}", [128, 1024], F32)) for i in range(2)]
    pac = [pac2[0][:, 0:512], pac2[0][:, 512:1024],
           pac2[1][:, 0:512], pac2[1][:, 512:1024]]
    pav = [ps2.enter_context(nc.psum_tensor(f"pav{i}", [128, 512], F32)) for i in range(2)]
    prs = [ps2.enter_context(nc.psum_tensor(f"prs{i}", [1, 512], F32)) for i in range(2)]

    NW = (SLOT_KTS[0] // WIN, SLOT_KTS[1] // WIN)

    def gw_of(sl, w):
        return (NW[0] if sl else 0) + w

    def p2_sync(env, eng):
        kTd = kT_s.rearrange("(h p) t -> p h t", p=128)
        vd = v_s.rearrange("(n p) d -> p n d", p=128)
        qd = qT_s.rearrange("p (h t) -> p h t", h=NH)
        avd = avn_s.rearrange("p (n c) -> p n c", n=NTT)

        def qload(sl):
            env.inc(eng.dma_start(out=qT_sl[sl][:, :].rearrange("p (h t) -> p h t", h=NH),
                                  in_=qd[:, :, sl * 512:(sl + 1) * 512]),
                    s_qsl, 16, mark=("p2q", sl))

        def windows(sl, ws):
            for w in ws:
                gw = gw_of(sl, w)
                kt0 = w * WIN
                if gw == 0:
                    env.wait(eng, s_spk, ("p1spk", NGRP - 1))
                    env.wait(eng, s_spr, ("p1spr", NGRP - 1))
                    env.wait(eng, s_spv, ("p1spv", NGRP - 1))
                with env.cond(eng, ("wsk", gw, "sy"), win_fl[0:1, gw:gw + 1]):
                    if gw >= 1:
                        env.wait(eng, s_phw, ("p2hw", gw - 1, NH - 1))
                    env.inc(eng.dma_start(out=krw_lo[gw % 2][:, :],
                                          in_=krT_s[0:32, kt0 * 128:(kt0 + WIN) * 128]),
                            s_krw[gw % 2], 16)
                    env.inc(eng.dma_start(out=krw_hi[gw % 2][:, :],
                                          in_=krT_s[32:64, kt0 * 128:(kt0 + WIN) * 128]),
                            s_krw[gw % 2], 16, mark=("p2kv", gw))
                    for hf in range(2):
                        hh = gw * 2 + hf
                        if hh >= 2:
                            pw, pf = (hh - 2) // 2, (hh - 2) % 2
                            env.wait(eng, s_phw, ("p2hw", pw, pf * 8 + 7))
                        env.inc(eng.dma_start(
                            out=ktw[hh % 2][:, :].rearrange("p (h t) -> p h t", h=8),
                            in_=kTd[:, hf * 8:(hf + 1) * 8, kt0 * 128:(kt0 + WIN) * 128]),
                            s_kvh[hh % 2], 16)
                        env.inc(eng.dma_start(
                            out=vw[hh % 2][:, :].rearrange("p (n c) -> p n c", n=WIN),
                            in_=vd[:, kt0:kt0 + WIN, hf * 1024:(hf + 1) * 1024]),
                            s_kvh[hh % 2], 16, mark=("p2kvh", gw, hf))

        def dance(sl):
            env.wait(eng, s_vrs, ("p2rs", gw_of(sl, NW[sl] - 1) * NH + NH - 1))
            c = env.inc(eng.dma_start(out=rs_d[sl:sl + 1, :], in_=rs_sb[:, :]),
                        s_rsp, 16)
            env.wait_val(eng, s_rsp, c)
            env.inc(eng.dma_start(
                out=rsp_t[:, :],
                in_=rs_d[sl:sl + 1, :].rearrange("o (p c) -> (o p) c", p=128)),
                s_rsp, 16, mark=("p2rsp", sl))
            env.wait(eng, s_vav, ("p2av", gw_of(sl, NW[sl] - 1) * NH + NH - 1))
            env.inc(eng.dma_start(out=av_s[sl * 128:(sl + 1) * 128, :],
                                  in_=av_sb[:, :]),
                    s_avnsp[sl], 16, mark=("p2avsp", sl))
            env.wait(eng, s_va, ("p2rcp", sl))
            env.inc(eng.dma_start(
                out=rs_d[sl:sl + 1, :].rearrange("o (p c) -> (o p) c", p=128),
                in_=rsp_t[:, :]), s_rsp, 16, mark=("p2rrb", sl))

        env.wait(eng, s_qsp[0], ("p0qsp", NH - 2))
        env.wait(eng, s_qsp[1], ("p0qsp", NH - 1))
        qload(0)
        qload(1)
        windows(0, range(NW[0]))
        windows(1, range(1))
        dance(0)
        windows(1, range(1, NW[1]))
        dance(1)

    def p2_tensor(env, eng):
        L = 6
        env.wait(eng, s_va, "qrT")
        for sl in range(2):
            kbase = SLOT_KTS[0] if sl else 0
            env.wait(eng, s_qsl, ("p2q", 1))
            for w in range(NW[sl]):
                gw = gw_of(sl, w)
                with env.cond(eng, ("wsk", gw, "te"), win_fl[0:1, gw:gw + 1]):
                 env.wait(eng, s_krw[gw % 2], ("p2kv", gw))
                 for kt in range(WIN):
                    gkt = kbase + w * WIN + kt
                    if kt < 4 and gw >= 1:
                        env.wait(eng, s_apr, ("p2pr", (gw * 128 - 4 + kt) | 1))
                    if kt >= 4:
                        env.wait(eng, s_aear, ("p2ear", gkt - 4))
                    eng.matmul(pac[kt % 4],
                               krw_lo[gw % 2][:, kt * 128:(kt + 1) * 128],
                               qrT_lo[:, sl * 512:(sl + 1) * 512],
                               start=True, stop=False)
                    env.inc(eng.matmul(pac[kt % 4],
                                       krw_hi[gw % 2][:, kt * 128:(kt + 1) * 128],
                                       qrT_hi[:, sl * 512:(sl + 1) * 512],
                                       start=False, stop=True),
                            s_ar, 1, mark=("p2ar", gkt))
                 for i in range(128 + L):
                    if i < 128:
                        h, kt = divmod(i, WIN)
                        gac = gw * 128 + i
                        hh = gw * 2 + h // 8
                        if i % 64 == 0:
                            env.wait(eng, s_kvh[hh % 2], ("p2kvh", gw, h // 8))
                        if i < 4:
                            env.wait(eng, s_aear, ("p2ear", kbase + w * WIN + 4 + i))
                        elif i % 2 == 0:
                            env.wait(eng, s_apr, ("p2pr", (gac - 4) | 1))
                        ins2 = eng.matmul(
                            pac[gac % 4],
                            ktw[hh % 2][:, (h % 8) * WIN * 128 + kt * 128: (h % 8) * WIN * 128 + (kt + 1) * 128],
                            qT_sl[sl][:, h * 512:(h + 1) * 512],
                            start=True, stop=True)
                        if i % 2 == 1 or i == 127:
                            env.inc(ins2, s_pac, 1, mark=("p2ac", gac))
                        else:
                            env.mark_here(s_pac, ("p2ac", gac))
                    if i >= L:
                        j = i - L
                        hj, ktj = divmod(j, WIN)
                        gacj = gw * 128 + j
                        ghwj = gw * NH + hj
                        hhj = gw * 2 + hj // 8
                        if ktj == 0 and ghwj >= 2:
                            env.wait(eng, s_vav, ("p2av", ghwj - 2))
                            env.wait(eng, s_vrs, ("p2rs", ghwj - 2))
                        if j % 2 == 0:
                            env.wait(eng, s_vp, ("p2p", (gacj + 1)))
                        pslice = p_buf[(gacj // 2) % 4][:, (gacj % 2) * 512:(gacj % 2) * 512 + 512]
                        eng.matmul(pav[hj % 2][:, :],
                                   vw[hhj % 2][:, ktj * 1024 + (hj % 8) * 128: ktj * 1024 + (hj % 8) * 128 + 128],
                                   pslice,
                                   start=(ktj == 0), stop=(ktj == WIN - 1))
                        ins = eng.matmul(prs[hj % 2][:, :], ones_b[:, :],
                                         pslice,
                                         start=(ktj == 0), stop=(ktj == WIN - 1))
                        if ktj == WIN - 1:
                            env.inc(ins, s_phw, 1, mark=("p2hw", gw, hj))

    def p2_scalar(env, eng):
        for sl in range(2):
            kbase = SLOT_KTS[0] if sl else 0
            for w in range(NW[sl]):
                gw = gw_of(sl, w)
                with env.cond(eng, ("wsk", gw, "ac"), win_fl[0:1, gw:gw + 1]):
                 for kt in range(WIN):
                    gkt = kbase + w * WIN + kt
                    env.wait(eng, s_ar, ("p2ar", gkt))
                    if gkt >= 1:
                        env.wait(eng, s_vear, ("p2em", gkt - 1))
                    env.inc(eng.activation(out=earR[0][:, :],
                                           in_=pac[kt % 4], func=AF.Exp),
                            s_aear, 1, mark=("p2ear", gkt))
                 for i in range(0, 128, 2):
                    gac = gw * 128 + i
                    env.wait(eng, s_pac, ("p2ac", gac + 1))
                    if gac >= 8:
                        env.wait(eng, s_vp, ("p2p", gac - 7))
                    env.inc(eng.activation(out=pr_buf[(gac // 2) % 4][:, :],
                                           in_=pac2[(gac // 2) % 2][:, :], func=AF.Exp),
                            s_apr, 1, mark=("p2pr", gac + 1))

    def p2_vector(env, eng):
        for sl in range(2):
            kbase = SLOT_KTS[0] if sl else 0
            for w in range(NW[sl]):
                gw = gw_of(sl, w)
                with env.cond(eng, ("wsk", gw, "ve"), win_fl[0:1, gw:gw + 1]):
                 for kt in range(WIN):
                    gkt = kbase + w * WIN + kt
                    eng.tensor_scalar(
                        out=mask_t[:, :], in0=qpos_bc[:, sl * 512:(sl + 1) * 512],
                        scalar1=float((w * WIN + kt) * 128), scalar2=iota_t[:, :],
                        op0=ALU.subtract, op1=ALU.is_ge)
                    env.wait(eng, s_aear, ("p2ear", gkt))
                    env.inc(eng.tensor_tensor(out=earM[:, kt * 512:(kt + 1) * 512],
                                              in0=earR[0][:, :],
                                              in1=mask_t[:, :], op=ALU.mult),
                            s_vear, 1, mark=("p2em", gkt))
                 for h in range(NH):
                    ghw = gw * NH + h
                    for kt in range(0, WIN, 2):
                        gac = ghw * WIN + kt
                        env.wait(eng, s_apr, ("p2pr", gac + 1))
                        env.inc(eng.tensor_tensor(out=p_buf[(gac // 2) % 4][:, :],
                                                  in0=pr_buf[(gac // 2) % 4][:, :],
                                                  in1=earM[:, kt * 512:(kt + 2) * 512],
                                                  op=ALU.mult),
                                s_vp, 2, mark=("p2p", gac + 1)),
                        env.mark_here(s_vp, ("p2p", gac))
                    env.wait(eng, s_phw, ("p2hw", gw, h))
                    if w == 0:
                        if sl == 1:
                            env.wait(eng, s_avnsp[0], ("p2avsp", 0))
                            if h == 0:
                                env.wait(eng, s_rsp, ("p2rsp", 0))
                        env.inc(eng.tensor_copy(out=av_sb[:, h * 512:(h + 1) * 512],
                                                in_=pav[h % 2][:, :]),
                                s_vav, 1, mark=("p2av", ghw))
                        env.inc(eng.tensor_copy(out=rs_sb[:, h * 512:(h + 1) * 512],
                                                in_=prs[h % 2][0:1, :]),
                                s_vrs, 1, mark=("p2rs", ghw))
                    else:
                        env.inc(eng.tensor_tensor(out=av_sb[:, h * 512:(h + 1) * 512],
                                                  in0=av_sb[:, h * 512:(h + 1) * 512],
                                                  in1=pav[h % 2][:, :], op=ALU.add),
                                s_vav, 1, mark=("p2av", ghw))
                        env.inc(eng.tensor_tensor(out=rs_sb[:, h * 512:(h + 1) * 512],
                                                  in0=rs_sb[:, h * 512:(h + 1) * 512],
                                                  in1=prs[h % 2][0:1, :], op=ALU.add),
                                s_vrs, 1, mark=("p2rs", ghw))
                if sl == 1 and w == 0:
                    env.wait(eng, s_rsp, ("p2rsp", 0))
                    rins = eng.reciprocal(out=rsp_t[:, :], in_=rsp_t[:, :])
                    c = env.inc(rins, s_va, 1, mark=("p2rcp", 0))
                    env.wait_val(eng, s_va, c)


    def p2_vector_tail(env, eng):
        env.wait(eng, s_rsp, ("p2rsp", 1))
        rins = eng.reciprocal(out=rsp_t[:, :], in_=rsp_t[:, :])
        c = env.inc(rins, s_va, 1, mark=("p2rcp", 1))
        env.wait_val(eng, s_va, c)

    _p2_vec = p2_vector

    def p2_vector(env, eng):
        _p2_vec(env, eng)
        p2_vector_tail(env, eng)

    run_phase("p2", {"sync": p2_sync, "tensor": p2_tensor,
                     "scalar": p2_scalar, "vector": p2_vector})
    ps2.close()
    es2.close()
    esQ.close()

    esDA = ExitStack()
    down_acc = esDA.enter_context(nc.sbuf_tensor("down_acc", [128, NTT * D], F32))

    es25 = ExitStack()
    sb25 = lambda name, shape, dt: es25.enter_context(nc.sbuf_tensor(name, shape, dt))
    wo_b = [sb25(f"wo_b{i}", [128, 16 * 512], BF16) for i in range(4)]
    av_tt = [sb25(f"av_tt{i}", [128, D], F32) for i in range(2)]
    avn_tt = [sb25(f"avn_tt{i}", [128, D], BF16) for i in range(2)]
    rr_tt = [sb25(f"rr_tt{i}", [1, D], F32) for i in range(2)]
    xo25 = [sb25(f"xo25_{i}", [128, 512], F32) for i in range(2)]

    ps25 = ExitStack()
    pW = [ps25.enter_context(nc.psum_tensor(f"pW{i}", [128, 512], F32)) for i in range(2)]
    pB = [ps25.enter_context(nc.psum_tensor(f"pB{i}", [128, 512], F32)) for i in range(2)]

    def p25_sync(env, eng):
        xov = x_own.rearrange("(n p) d -> p n d", p=128)
        for tt in range(NTT):
            sl, tl = tt // 4, tt % 4
            if tt >= 2:
                env.wait(eng, s_va, ("p25n", tt - 2, 3))
            env.wait(eng, s_avnsp[sl], ("p2avsp", sl))
            env.inc(eng.dma_start(
                out=av_tt[tt % 2][:, :].rearrange("p (h c) -> p h c", h=NH),
                in_=av_s.rearrange("(s p) (h q) -> s p h q", s=2, h=NH)[
                    sl, :, :, tl * 128:(tl + 1) * 128]),
                s_a25[tt % 2], 16)
            env.wait(eng, s_rsp, ("p2rrb", sl))
            env.inc(eng.dma_start(
                out=rr_tt[tt % 2][:, :].rearrange("o (h c) -> o h c", h=NH),
                in_=rs_d.rearrange("s (h q) -> s h q", h=NH)[
                    sl, :, tl * 128:(tl + 1) * 128].unsqueeze(0)),
                s_a25[tt % 2], 16, mark=("p25l", tt))
            for dg in range(4):
                i = tt * 4 + dg
                if i >= 2:
                    env.wait(eng, s_vd, ("p25", i - 2))
                env.inc(eng.dma_start(out=xo25[i % 2][:, :],
                                      in_=xov[:, tt, dg * 512:(dg + 1) * 512]),
                        s_xo[i % 2], 16, mark=("p25x", i))

    def p25_gpsimd(env, eng):
        wov = Wo.rearrange("(a p) d -> p a d", p=128)
        for dg in range(4):
            if dg >= 2:
                env.wait(eng, s_wo[dg % 2], ("p25w", dg - 2))
            env.inc(eng.dma_start(
                out=wo_b[dg][:, :].rearrange("p (a c) -> p a c", a=16),
                in_=wov[:, :, dg * 512:(dg + 1) * 512]),
                s_wo[dg % 2], 16, mark=("p25w", dg))

    def p25_tensor(env, eng):
        for tt in range(NTT):
            env.wait(eng, s_a25[tt % 2], ("p25l", tt))
            for quad in range(4):
                if tt * 4 + quad >= 2:
                    env.wait(eng, s_va, ("p25n", *divmod(tt * 4 + quad - 2, 4)))
                env.inc(eng.matmul(pB[quad % 2][:, :], ones_f1[:, :],
                                   rr_tt[tt % 2][:, quad * 512:(quad + 1) * 512],
                                   start=True, stop=True),
                        s_pbc, 1, mark=("p25b", tt, quad))
            for dg in range(4):
                i = tt * 4 + dg
                env.wait(eng, s_wo[dg % 2], ("p25w", dg))
                env.wait(eng, s_va, ("p25n", tt, 3))
                if i >= 2:
                    env.wait(eng, s_vd, ("p25", i - 2))
                ins = None
                for ft in range(16):
                    ins = eng.matmul(pW[i % 2][:, :],
                                     avn_tt[tt % 2][:, ft * 128:(ft + 1) * 128],
                                     wo_b[dg][:, ft * 512:(ft + 1) * 512],
                                     start=(ft == 0), stop=(ft == 15))
                env.inc(ins, s_pac, 1, mark=("p25m", i))

    def p25_vector(env, eng):
        for tt in range(NTT):
            if tt >= 2:
                env.wait(eng, s_pac, ("p25m", (tt - 2) * 4 + 3))
            for quad in range(4):
                env.wait(eng, s_pbc, ("p25b", tt, quad))
                env.inc(eng.tensor_tensor(
                    out=avn_tt[tt % 2][:, quad * 512:(quad + 1) * 512],
                    in0=av_tt[tt % 2][:, quad * 512:(quad + 1) * 512],
                    in1=pB[quad % 2][:, :], op=ALU.mult),
                    s_va, 1, mark=("p25n", tt, quad))
            for dg in range(4):
                i = tt * 4 + dg
                env.wait(eng, s_pac, ("p25m", i))
                env.wait(eng, s_xo[i % 2], ("p25x", i))
                env.inc(eng.tensor_tensor(
                    out=down_acc[:, tt * D + dg * 512: tt * D + (dg + 1) * 512],
                    in0=pW[i % 2][:, :], in1=xo25[i % 2][:, :], op=ALU.add),
                    s_vd, 1, mark=("p25", i))

    run_phase("p25", {"sync": p25_sync, "gpsimd": p25_gpsimd,
                      "tensor": p25_tensor, "vector": p25_vector})
    ps25.close()
    es25.close()

    es3 = ExitStack()
    sb3 = lambda name, shape, dt: es3.enter_context(nc.sbuf_tensor(name, shape, dt))
    xnT3 = sb3("xnT3", [128, 16 * NTOK], BF16)
    h_gb = [sb3(f"h_gb{i}", [128, 1024], F32) for i in range(NTT)]
    hs_st = [sb3(f"hs_st{i}", [128, 1024], BF16) for i in range(2)]
    hsT_gb = sb3("hsT_gb", [128, 8 * NTOK], BF16)
    wgub = [sb3(f"wgub{i}", [128, 16 * 512], BF16) for i in range(2)]
    wdb = [sb3(f"wdb{i}", [128, 8 * 512], BF16) for i in range(2)]
    sg_t = [sb3(f"sg_t{i}", [128, 256], F32) for i in range(2)]
    A_t = [sb3(f"A_t{i}", [128, 256], F32) for i in range(4)]
    wc_t = [sb3(f"wc_t{i}", [128, 256], BF16) for i in range(2)]
    ss_t = sb3("ss_t", [128, 256], BF16)

    ps3 = ExitStack()
    pGU = [ps3.enter_context(nc.psum_tensor(f"pGU{i}", [128, 512], F32)) for i in range(2)]
    pT3 = [ps3.enter_context(nc.psum_tensor(f"pT3_{i}", [128, 512], BF16)) for i in range(2)]
    pD = [ps3.enter_context(nc.psum_tensor(f"pD{i}", [128, 512], F32)) for i in range(2)]

    def p3_sync(env, eng):
        env.wait(eng, s_xnt, "xnT")
        env.inc(eng.dma_start(out=xnT3[:, :], in_=xnT_s), s_xnt3, 16, mark="p3xnT")

    def p3_gpsimd(env, eng):
        wgv = wg_p.rearrange("(a p) i -> p a i", p=128)
        wuv = wu_p.rearrange("(a p) i -> p a i", p=128)
        wdv = wd_p.rearrange("(ph gb s p) d -> p ph gb s d", ph=4, gb=NGB, s=2, p=128)
        for gb in range(NGB):
            for ph in range(4):
                i = gb * 4 + ph
                c0 = ph * 2048 + gb * 256
                if i >= 2:
                    env.wait(eng, s_pac, ("p3u", i - 2, NTT - 1))
                wv = wgub[i % 2][:, :].rearrange("p (a h c) -> p a h c", a=16, h=2)
                env.inc(eng.dma_start(out=wv[:, :, 0, :],
                                      in_=wgv[:, :, c0:c0 + 256]), s_w3[i % 2], 16)
                env.inc(eng.dma_start(out=wv[:, :, 1, :],
                                      in_=wuv[:, :, c0:c0 + 256]), s_w3[i % 2], 16,
                        mark=("p3w", i))
            for dg in range(4):
                i = gb * 4 + dg
                if dg < 2:
                    if gb >= 1:
                        env.wait(eng, s_pac, ("p3d", gb - 1, 31))
                else:
                    env.wait(eng, s_pac, ("p3d", gb, 15))
                for ph in range(4):
                    env.inc(eng.dma_start(
                        out=wdb[i % 2][:, ph * 1024:(ph + 1) * 1024].rearrange(
                            "p (a c) -> p a c", a=2),
                        in_=wdv[:, ph, gb, :, dg * 512:(dg + 1) * 512]),
                        s_wd, 16, mark=("p3wd", i) if ph == 3 else None)

    def p3_tensor(env, eng):
        env.wait(eng, s_xnt3, "p3xnT")
        for gb in range(NGB):
            for ph in range(4):
                i = gb * 4 + ph
                env.wait(eng, s_w3[i % 2], ("p3w", i))
                for tt in range(NTT):
                    gidx = i * NTT + tt
                    if gidx >= 2:
                        env.wait(eng, s_asl, ("p3sl", gidx - 2))
                        env.wait(eng, s_vh, ("p3h", gidx - 2))
                    ins = None
                    for dt_i in range(16):
                        ins = eng.matmul(
                            pGU[gidx % 2][:, :],
                            xnT3[:, dt_i * NTOK + tt * 128: dt_i * NTOK + (tt + 1) * 128],
                            wgub[i % 2][:, dt_i * 512:(dt_i + 1) * 512],
                            start=(dt_i == 0), stop=(dt_i == 15))
                    env.inc(ins, s_pac, 1, mark=("p3u", i, tt))
            def down_sess(ds, dg, tt):
                i = gb * 4 + dg
                env.wait(eng, s_wd, ("p3wd", i))
                gds = gb * 32 + ds
                if gds >= 2:
                    env.wait(eng, s_vd, ("p3dv", gds - 2))
                ins = None
                for it in range(8):
                    ins = eng.matmul(
                        pD[gds % 2][:, :],
                        hsT_gb[:, it * NTOK + tt * 128: it * NTOK + (tt + 1) * 128],
                        wdb[i % 2][:, it * 512:(it + 1) * 512],
                        start=(it == 0), stop=(it == 7))
                env.inc(ins, s_pac, 1, mark=("p3d", gb, ds))

            for tt in range(NTT):
                env.wait(eng, s_vhs, ("p3hs", gb, tt))
                for j in range(2):
                    gi = (gb * NTT + tt) * 2 + j
                    if gi >= 2:
                        pv = gi - 2
                        env.wait(eng, s_aev, ("p3evT", pv // 2, pv % 2))
                    ins = None
                    for jj in range(4):
                        it = j * 4 + jj
                        ins = eng.transpose(pT3[gi % 2][:, jj * 128:(jj + 1) * 128],
                                            hs_st[tt % 2][:, it * 128:(it + 1) * 128],
                                            ident_t[:, :])
                    env.inc(ins, s_ptr, 1, mark=("p3tr", gb * NTT + tt, j))
                env.wait(eng, s_aev, ("p3evT", gb * NTT + tt, 1))
                for dgl in range(2):
                    down_sess(tt * 2 + dgl, dgl, tt)
            for dg in range(2, 4):
                for tt in range(NTT):
                    down_sess(16 + (dg - 2) * 8 + tt, dg, tt)

    def p3_scalar(env, eng):
        for gb in range(NGB):
            for ph in range(4):
                i = gb * 4 + ph
                for tt in range(NTT):
                    gidx = i * NTT + tt
                    env.wait(eng, s_pac, ("p3u", i, tt))
                    if gidx >= 2:
                        env.wait(eng, s_vh, ("p3h", gidx - 2))
                    env.inc(eng.activation(out=sg_t[gidx % 2][:, :],
                                           in_=pGU[gidx % 2][:, 0:256], func=AF.Silu),
                            s_asl, 1, mark=("p3sl", gidx))

            def squares(tt):
                env.wait(eng, s_vh, ("p3h", (gb * 4 + 3) * NTT + tt))
                if tt >= 1:
                    env.wait(eng, s_vhs, ("p3hs", gb, tt - 1))
                ins = None
                for pp in range(4):
                    ins = eng.activation(out=A_t[pp][:, :],
                                         in_=h_gb[tt][:, pp * 256:(pp + 1) * 256],
                                         func=AF.Square)
                env.inc(ins, s_asl, 1, mark=("p3A", gb, tt))

            def evacs(tt):
                for j in range(2):
                    gi = (gb * NTT + tt) * 2 + j
                    env.wait(eng, s_ptr, ("p3tr", gb * NTT + tt, j))
                    if tt == 0 and j == 0 and gb >= 1:
                        env.wait(eng, s_pac, ("p3d", gb - 1, 31))
                    dst = hsT_gb[:, :].rearrange("p (a n t) -> p a n t", a=8, n=NTT)
                    env.inc(eng.copy(out=dst[:, j * 4:(j + 1) * 4, tt, :],
                                     in_=pT3[gi % 2][:, :].rearrange(
                                         "p (a t) -> p a t", a=4)),
                            s_aev, 1, mark=("p3evT", gb * NTT + tt, j))

            squares(0)
            for tt in range(1, NTT):
                squares(tt)
                evacs(tt - 1)
            evacs(NTT - 1)

    def p3_vector(env, eng):
        for gb in range(NGB):
            for ph in range(4):
                i = gb * 4 + ph
                for tt in range(NTT):
                    gidx = i * NTT + tt
                    env.wait(eng, s_asl, ("p3sl", gidx))
                    if ph == 0 and gb >= 1:
                        env.wait(eng, s_asl, ("p3A", gb - 1, tt))
                    env.inc(eng.tensor_tensor(
                        out=h_gb[tt][:, ph * 256:(ph + 1) * 256],
                        in0=sg_t[gidx % 2][:, :], in1=pGU[gidx % 2][:, 256:512],
                        op=ALU.mult), s_vh, 1, mark=("p3h", gidx))

            def sparsify(tt):
                env.wait(eng, s_asl, ("p3A", gb, tt))
                if gb * NTT + tt >= 2:
                    env.wait(eng, s_ptr, ("p3tr", gb * NTT + tt - 2, 1))
                b01, b02, b03, b12, b13, b23 = [(0, 1), (0, 2), (0, 3),
                                                (1, 2), (1, 3), (2, 3)]

                def cmp(dst, pr):
                    eng.tensor_tensor(out=dst[:, :], in0=A_t[pr[0]][:, :],
                                      in1=A_t[pr[1]][:, :], op=ALU.is_ge)

                def keep(i4, sacc, thr, opk):
                    return eng.scalar_tensor_tensor(
                        out=hs_st[tt % 2][:, i4 * 256:(i4 + 1) * 256],
                        in0=sacc[:, :], scalar=thr,
                        in1=h_gb[tt][:, i4 * 256:(i4 + 1) * 256],
                        op0=opk, op1=ALU.mult)

                wcA, wcB = wc_t
                cmp(wcA, b01)
                cmp(wcB, b02)
                eng.tensor_tensor(out=ss_t[:, :], in0=wcA[:, :], in1=wcB[:, :], op=ALU.add)
                cmp(wcB, b03)
                eng.tensor_tensor(out=ss_t[:, :], in0=ss_t[:, :], in1=wcB[:, :], op=ALU.add)
                keep(0, ss_t, 2.0, ALU.is_ge)
                cmp(wcA, b13)
                eng.tensor_tensor(out=ss_t[:, :], in0=wcB[:, :], in1=wcA[:, :], op=ALU.add)
                cmp(wcB, b23)
                eng.tensor_tensor(out=ss_t[:, :], in0=ss_t[:, :], in1=wcB[:, :], op=ALU.add)
                keep(3, ss_t, 1.0, ALU.is_le)
                cmp(wcB, b12)
                eng.tensor_tensor(out=ss_t[:, :], in0=wcA[:, :], in1=wcB[:, :], op=ALU.add)
                cmp(wcA, b01)
                eng.tensor_tensor(out=ss_t[:, :], in0=ss_t[:, :], in1=wcA[:, :], op=ALU.subtract)
                keep(1, ss_t, 1.0, ALU.is_ge)
                cmp(wcA, b23)
                eng.tensor_tensor(out=ss_t[:, :], in0=wcA[:, :], in1=wcB[:, :], op=ALU.subtract)
                cmp(wcB, b02)
                ins = None
                eng.tensor_tensor(out=ss_t[:, :], in0=ss_t[:, :], in1=wcB[:, :], op=ALU.subtract)
                ins = keep(2, ss_t, 0.0, ALU.is_ge)
                env.inc(ins, s_vhs, 1, mark=("p3hs", gb, tt))

            def down_add(ds, dg, tt):
                gds = gb * 32 + ds
                env.wait(eng, s_pac, ("p3d", gb, ds))
                env.inc(eng.tensor_tensor(
                    out=down_acc[:, tt * D + dg * 512: tt * D + (dg + 1) * 512],
                    in0=down_acc[:, tt * D + dg * 512: tt * D + (dg + 1) * 512],
                    in1=pD[gds % 2][:, :], op=ALU.add),
                    s_vd, 1, mark=("p3dv", gds))

            for tt in range(NTT):
                sparsify(tt)
                if tt >= 1:
                    for dgl in range(2):
                        down_add((tt - 1) * 2 + dgl, dgl, tt - 1)
            for dgl in range(2):
                down_add((NTT - 1) * 2 + dgl, dgl, NTT - 1)
            for dg in range(2, 4):
                for tt in range(NTT):
                    down_add(16 + (dg - 2) * 8 + tt, dg, tt)

    run_phase("p3", {"sync": p3_sync, "gpsimd": p3_gpsimd, "tensor": p3_tensor,
                     "scalar": p3_scalar, "vector": p3_vector})
    ps3.close()
    es3.close()

    def p4_sync(env, eng):
        ov = out_own.rearrange("(n p) d -> p n d", p=128)
        for tt in range(NTT):
            env.wait(eng, s_vd, ("p3dv", (NGB - 1) * 32 + 16 + 8 + tt))
            env.inc(eng.dma_start(out=ov[:, tt, :],
                                  in_=down_acc[:, tt * D:(tt + 1) * D]),
                    s_out, 16, mark=("p4", tt))
        env.wait(eng, s_out, ("p4", NTT - 1))

    run_phase("p4", {"sync": p4_sync})
    esDA.close()
    es.close()
    return nc


def _host_prep(inputs):
    x = np.ascontiguousarray(np.asarray(inputs["L0_attn_x"]).reshape(T, D))
    w = {k: np.ascontiguousarray(np.asarray(v)) for k, v in inputs.items()}
    inv = 10000.0 ** (-np.arange(0, DR, 2, dtype=np.float64) / DR)
    f = np.outer(np.arange(T, dtype=np.float64), inv)
    coskf = np.cos(f).T.astype(np.float32)
    sinkf = np.sin(f).T.astype(np.float32)
    import ml_dtypes
    iota = np.arange(128, dtype=np.float32).reshape(128, 1)
    identb = np.eye(128, dtype=np.float32).astype(ml_dtypes.bfloat16)
    wg = w["L0_ffn_w_gate"]
    wu = w["L0_ffn_w_up"]
    wd = w["L0_ffn_w_down"]
    wg_p = np.ascontiguousarray(wg.reshape(D, FI // 4, 4).transpose(0, 2, 1).reshape(D, FI))
    wu_p = np.ascontiguousarray(wu.reshape(D, FI // 4, 4).transpose(0, 2, 1).reshape(D, FI))
    wd_p = np.ascontiguousarray(wd.reshape(FI // 4, 4, D).transpose(1, 0, 2).reshape(FI, D))

    shared = {
        "x_full": x, "cosk": coskf, "sink": sinkf, "iota": iota, "identb": identb,
        "Wkv": w["L0_attn_Wkv"], "Wkup": w["L0_attn_Wkup"], "Wvup": w["L0_attn_Wvup"],
        "Wq": w["L0_attn_Wq"], "Wqr": w["L0_attn_Wqr"], "Wkr": w["L0_attn_Wkr"],
        "Wo": w["L0_attn_Wo"], "wg_p": wg_p, "wu_p": wu_p, "wd_p": wd_p,
    }
    in_maps = []
    for c in range(NCORES):
        cA, cB = c, 15 - c
        rows = np.r_[cA * 512:(cA + 1) * 512, cB * 512:(cB + 1) * 512]
        m = dict(shared)
        m["x_own"] = np.ascontiguousarray(x[rows])
        m["qpos"] = rows.astype(np.float32).reshape(1, NTOK)
        m["cosq"] = np.ascontiguousarray(coskf[:, rows])
        m["sinq"] = np.ascontiguousarray(sinkf[:, rows])
        wf = np.zeros((1, 16), np.int32)
        wf[0, :(cA + 2) // 2] = 1
        wf[0, 4:4 + (cB + 2) // 2] = 1
        m["win_en"] = wf
        in_maps.append(m)
    return in_maps


def kernel(**inputs):
    in_maps = _host_prep(inputs)
    nc = build_program()
    res = run_bass_kernel_spmd(nc, in_maps, core_ids=list(range(NCORES)))
    out = np.empty((T, D), dtype=np.float32)
    for c in range(NCORES):
        oo = res.results[c]["out_own"]
        cA, cB = c, 15 - c
        out[cA * 512:(cA + 1) * 512] = oo[:512]
        out[cB * 512:(cB + 1) * 512] = oo[512:]
    if _DEBUG:
        kernel._dbg = res.results
    return out.reshape(1, T, D)


# version 18
# speedup vs baseline: 1.2222x; 1.2222x over previous
import os
import numpy as np
from contextlib import ExitStack, contextmanager

import concourse.bass as bass
import concourse.mybir as mybir
from concourse.bass_utils import run_bass_kernel_spmd

F32 = mybir.dt.float32
BF16 = mybir.dt.bfloat16
I32 = mybir.dt.int32
AF = mybir.ActivationFunctionType
ALU = mybir.AluOpType
AX = mybir.AxisListType.X

T, D, DC, DR, NH = 8192, 2048, 512, 64, 16
FI = 8192
NTOK = 1024
NCORES = 8
NTT = 8
NGRP = 16
SLOT_KTS = (32, 64)
WIN = 8
NGB = 8
SCALE = float(1.0 / np.sqrt(128 + 64))
EPS = 1e-5
INV_D = float(1.0 / D)

_DEBUG = bool(int(os.environ.get("MK_DEBUG", "0")))
_PHASES = os.environ.get("MK_PHASES", "all")


class _DummyInstr:
    def then_inc(self, *a, **k):
        return self


class _DummyEng:
    def __getattr__(self, name):
        if name == "wait_ge":
            return lambda *a, **k: None
        return lambda *a, **k: _DummyInstr()


class Env:

    def __init__(self):
        self.marks = {}
        self.cnt = {}
        self.passno = 1
        self.cond_deltas = {}
        self.sem_reg = {}

    def begin_pass(self, passno):
        self.passno = passno
        if passno == 1:
            self.snapshot = dict(self.cnt)
        else:
            self.final = dict(self.cnt)
            self.cnt = dict(self.snapshot)

    def end_phase_check(self):
        if self.passno == 2:
            assert self.cnt == self.final, (
                f"pass mismatch: {self.cnt} vs {self.final}")

    def inc(self, instr, sem, k=1, mark=None):
        instr.then_inc(sem.h, k) if self.passno == 2 else None
        self.cnt[sem.name] = self.cnt.get(sem.name, 0) + k
        if mark is not None:
            key = (sem.name, mark)
            if self.passno == 1:
                self.marks[key] = self.cnt[sem.name]
            else:
                assert self.marks[key] == self.cnt[sem.name], key
        return self.cnt[sem.name]

    def mark_here(self, sem, mark):
        key = (sem.name, mark)
        cur = self.cnt.get(sem.name, 0)
        if self.passno == 1:
            self.marks[key] = cur
        else:
            assert self.marks[key] == cur, key

    def wait(self, eng, sem, mark):
        if self.passno == 2:
            eng.wait_ge(sem.h, self.marks[(sem.name, mark)])

    def wait_val(self, eng, sem, val):
        if self.passno == 2:
            eng.wait_ge(sem.h, val)

    @contextmanager
    def cond(self, eng, key, flag_ap):
        if self.passno == 1:
            snap = dict(self.cnt)
            yield
            self.cond_deltas[key] = {
                k: v - snap.get(k, 0)
                for k, v in self.cnt.items() if v != snap.get(k, 0)}
        else:
            rv = eng.value_load(flag_ap)
            with eng.If(rv):
                yield
            with eng.Else():
                for sname, d in sorted(self.cond_deltas[key].items()):
                    eng.sem_inc(self.sem_reg[sname].h, d)


class Sem:
    def __init__(self, es, nc, name):
        self.h = es.enter_context(nc.semaphore(name))
        self.name = name


def build_program():
    nc = bass.Bass("TRN2", target_bir_lowering=False,
                   detect_race_conditions=False)
    env = Env()

    def inp(name, shape, dt=F32):
        return nc.dram_tensor(name, shape, dt, kind="ExternalInput").ap()

    x_own = inp("x_own", [NTOK, D])
    x_full = inp("x_full", [T, D])
    qpos = inp("qpos", [1, NTOK])
    cosq = inp("cosq", [32, NTOK])
    sinq = inp("sinq", [32, NTOK])
    cosk = inp("cosk", [32, T])
    sink = inp("sink", [32, T])
    iota = inp("iota", [128, 1])
    identb = inp("identb", [128, 128], BF16)
    win_en = inp("win_en", [1, 16], I32)
    Wkv = inp("Wkv", [D, DC])
    Wkup = inp("Wkup", [DC, D])
    Wvup = inp("Wvup", [DC, D])
    Wq = inp("Wq", [D, D])
    Wqr = inp("Wqr", [D, DR])
    Wkr = inp("Wkr", [DC, DR])
    Wo = inp("Wo", [D, D])
    wg_p = inp("wg_p", [D, FI])
    wu_p = inp("wu_p", [D, FI])
    wd_p = inp("wd_p", [FI, D])
    out_own = nc.dram_tensor("out_own", [NTOK, D], F32, kind="ExternalOutput").ap()

    skind = "ExternalOutput" if _DEBUG else "Internal"
    kT_s = nc.dram_tensor("kT_s", [D, T], BF16, kind=skind).ap()
    v_s = nc.dram_tensor("v_s", [T, D], BF16, kind=skind).ap()
    krT_s = nc.dram_tensor("krT_s", [DR, T], BF16, kind=skind).ap()
    xnT_s = nc.dram_tensor("xnT_s", [128, 16 * NTOK], BF16, kind=skind).ap()
    qT_s = nc.dram_tensor("qT_s", [128, NH * NTOK], BF16, kind=skind).ap()
    avn_s = nc.dram_tensor("avn_s", [128, NH * NTOK], BF16, kind=skind).ap()
    rs_d = nc.dram_tensor("rs_d", [2, NH * 512], F32, kind="Internal").ap()
    av_s = nc.dram_tensor("av_s", [2 * 128, NH * 512], F32, kind="Internal").ap()

    es = ExitStack()
    sb = lambda name, shape, dt: es.enter_context(nc.sbuf_tensor(name, shape, dt))
    iota_t = sb("iota_t", [128, 1], F32)
    ident_t = sb("ident_t", [128, 128], BF16)
    win_fl = sb("win_fl", [1, 16], I32)
    ones_b = sb("ones_b", [128, 1], BF16)
    ones_f1 = sb("ones_f1", [1, 128], F32)
    epst = sb("epst", [128, 1], F32)
    esQ = ExitStack()
    sbQ = lambda name, shape, dt: esQ.enter_context(nc.sbuf_tensor(name, shape, dt))
    qpos_bc = sbQ("qpos_bc", [128, NTOK], F32)
    qrT_lo = sbQ("qrT_lo", [32, NTOK], BF16)
    qrT_hi = sbQ("qrT_hi", [32, NTOK], BF16)

    def S(name):
        s = Sem(es, nc, name)
        env.sem_reg[name] = s
        return s
    s_ld = S("s_ld"); s_w = S("s_w")
    s_xo = [S(f"s_xo{i}") for i in range(2)]
    s_qsp = [S(f"s_qsp{i}") for i in range(2)]
    s_wq = [S(f"s_wq{i}") for i in range(2)]
    s_xnt = S("s_xnt")
    s_xs = [S(f"s_xs{i}") for i in range(6)]
    s_cs = [S(f"s_cs{i}") for i in range(2)]
    s_spk = S("s_spk"); s_spr = S("s_spr"); s_spv = S("s_spv")
    s_qsl = S("s_qsl")
    s_krw = [S(f"s_krw{i}") for i in range(2)]
    s_kvh = [S(f"s_kvh{i}") for i in range(2)]
    s_avnsp = [S(f"s_avnsp{i}") for i in range(2)]
    s_wo = [S(f"s_wo{i}") for i in range(2)]
    s_a25 = [S(f"s_a25{i}") for i in range(2)]
    s_w3 = [S(f"s_w3{i}") for i in range(2)]
    s_wd = S("s_wd"); s_xnt3 = S("s_xnt3"); s_ms = S("s_ms"); s_rsp = S("s_rsp")
    s_pq = S("s_pq"); s_ptr = S("s_ptr"); s_pac = S("s_pac")
    s_ar = S("s_ar"); s_phw = S("s_phw"); s_pbc = S("s_pbc")
    s_aev = S("s_aev"); s_apr = S("s_apr"); s_asq = S("s_asq"); s_astd = S("s_astd")
    s_aear = S("s_aear"); s_asl = S("s_asl")
    s_v = S("s_v"); s_vt = S("s_vt"); s_va = S("s_va"); s_vp = S("s_vp")
    s_vear = S("s_vear"); s_vav = S("s_vav"); s_vrs = S("s_vrs"); s_vavn = S("s_vavn")
    s_vh = S("s_vh"); s_vhs = S("s_vhs"); s_vd = S("s_vd"); s_out = S("s_out")

    def fence(eng, instr):
        c = env.inc(instr, s_vt, 1)
        env.wait_val(eng, s_vt, c)

    def run_phase(name, bodies):
        env.begin_pass(1)
        for eng_name in ("sync", "gpsimd", "tensor", "scalar", "vector"):
            if eng_name in bodies:
                bodies[eng_name](env, _DummyEng())
        env.begin_pass(2)
        with nc.Block(name) as blk:
            for eng_name in ("sync", "gpsimd", "tensor", "scalar", "vector"):
                if eng_name in bodies:
                    getattr(blk, eng_name)(
                        lambda eng, f=bodies[eng_name]: f(env, eng))
        env.end_phase_check()

    esW = ExitStack()
    sbW = lambda name, shape, dt: esW.enter_context(nc.sbuf_tensor(name, shape, dt))
    wkv_b = sbW("wkv_b", [128, 16 * DC], BF16)
    wkup_b = sbW("wkup_b", [128, 4 * D], BF16)
    wvup_b = sbW("wvup_b", [128, 4 * D], BF16)
    wqr_b = sbW("wqr_b", [128, 16 * DR], BF16)
    wkr_b = sbW("wkr_b", [128, 4 * DR], BF16)

    es0 = ExitStack()
    sb0 = lambda name, shape, dt: es0.enter_context(nc.sbuf_tensor(name, shape, dt))
    wqh = [sb0(f"wqh{i}", [128, 16 * 128], BF16) for i in range(2)]
    xnT_own = sb0("xnT_own", [128, 16 * NTOK], BF16)
    qst = [sb0(f"qst{i}", [128, NTOK], BF16) for i in range(2)]
    qpos_t = sb0("qpos_t", [1, NTOK], F32)
    cq_t = sb0("cq_t", [32, NTOK], F32)
    sq_t = sb0("sq_t", [32, NTOK], F32)
    xo = [sb0(f"xo{i}", [128, D], F32) for i in range(2)]
    xn_st = [sb0(f"xn_st{i}", [128, D], BF16) for i in range(2)]
    sq_scr = sb0("sq_scr", [128, D], BF16)
    sums0 = [sb0(f"sums0{i}", [128, 1], F32) for i in range(2)]
    sqa0 = [sb0(f"sqa0{i}", [128, 1], F32) for i in range(2)]
    mt0 = sb0("mt0", [128, 1], F32)
    vt0 = sb0("vt0", [128, 1], F32)
    m20 = sb0("m20", [128, 1], F32)
    std0 = [sb0(f"std0{i}", [128, 1], F32) for i in range(2)]
    rstd0 = sb0("rstd0", [128, 1], F32)
    qr_lo = sb0("qr_lo", [32, NTOK], F32)
    qr_hi = sb0("qr_hi", [32, NTOK], F32)
    rt1 = sb0("rt1", [32, NTOK], F32)
    rt2 = sb0("rt2", [32, NTOK], F32)

    ps0 = ExitStack()
    pT = [ps0.enter_context(nc.psum_tensor(f"pT{i}", [128, 512], BF16)) for i in range(2)]
    pP = [ps0.enter_context(nc.psum_tensor(f"pP{i}", [128, 512], F32)) for i in range(2)]
    pQR = [ps0.enter_context(nc.psum_tensor(f"pQR{i}", [32, 512], F32)) for i in range(2)]

    def p0_sync(env, eng):
        env.inc(eng.dma_start(out=iota_t[:, :], in_=iota), s_ld, 16)
        env.inc(eng.dma_start(out=ident_t[:, :], in_=identb), s_ld, 16)
        env.inc(eng.dma_start(out=qpos_t[:, :], in_=qpos), s_ld, 16)
        env.inc(eng.dma_start(out=cq_t[:, :], in_=cosq), s_ld, 16)
        env.inc(eng.dma_start(out=sq_t[:, :], in_=sinq), s_ld, 16)
        env.inc(eng.dma_start(out=win_fl[:, :], in_=win_en), s_ld, 16, mark="consts")
        xov = x_own.rearrange("(n p) d -> p n d", p=128)
        for tt in range(NTT):
            if tt >= 2:
                env.wait(eng, s_v, ("p0xn", tt - 2))
                env.wait(eng, s_asq, ("p0sq", tt - 2))
            env.inc(eng.dma_start(out=xo[tt % 2][:, :], in_=xov[:, tt, :]),
                    s_xo[tt % 2], 16, mark=("p0x", tt))
        env.wait(eng, s_aev, ("p0ev", NTT - 1, 3))
        env.inc(eng.dma_start(out=xnT_s, in_=xnT_own[:, :]), s_xnt, 16, mark="xnT")
        for h in range(NH):
            env.wait(eng, s_apr, ("p0q", h, 1))
            env.inc(eng.dma_start(out=qT_s[:, h * NTOK:(h + 1) * NTOK],
                                  in_=qst[h % 2][:, :]),
                    s_qsp[h % 2], 16, mark=("p0qsp", h))

    def p0_gpsimd(env, eng):
        eng.memset(ones_b[:, :], 1.0)
        eng.memset(ones_f1[:, :], 1.0)
        env.inc(eng.memset(epst[:, :], EPS), s_ms, 1, mark="consts")
        env.inc(eng.dma_start(out=wkv_b[:, :].rearrange("p (a f) -> p a f", a=16),
                              in_=Wkv.rearrange("(a p) f -> p a f", p=128)), s_w, 16)
        env.inc(eng.dma_start(out=wkup_b[:, :].rearrange("p (a f) -> p a f", a=4),
                              in_=Wkup.rearrange("(a p) f -> p a f", p=128)), s_w, 16)
        env.inc(eng.dma_start(out=wvup_b[:, :].rearrange("p (a f) -> p a f", a=4),
                              in_=Wvup.rearrange("(a p) f -> p a f", p=128)), s_w, 16)
        env.inc(eng.dma_start(out=wqr_b[:, :].rearrange("p (a f) -> p a f", a=16),
                              in_=Wqr.rearrange("(a p) f -> p a f", p=128)), s_w, 16)
        env.inc(eng.dma_start(out=wkr_b[:, :].rearrange("p (a f) -> p a f", a=4),
                              in_=Wkr.rearrange("(a p) f -> p a f", p=128)),
                s_w, 16, mark="attnw")
        wqv = Wq.rearrange("(a p) f -> p a f", p=128)
        for h in range(NH):
            if h >= 2:
                env.wait(eng, s_pq, ("p0qm", h - 2, 1))
            env.inc(eng.dma_start(
                out=wqh[h % 2][:, :].rearrange("p (a f) -> p a f", a=16),
                in_=wqv[:, :, h * 128:(h + 1) * 128]),
                s_wq[h % 2], 16, mark=("p0wq", h))

    def p0_tensor(env, eng):
        env.wait(eng, s_ms, "consts")
        env.wait(eng, s_ld, "consts")
        for s in range(2):
            env.inc(eng.matmul(pP[s][:, :], ones_f1[:, :],
                               qpos_t[:, s * 512:(s + 1) * 512],
                               start=True, stop=True), s_pq, 1, mark=("p0bc", s))
        for tt in range(NTT):
            env.wait(eng, s_v, ("p0xn", tt))
            for j in range(4):
                gidx = tt * 4 + j
                if gidx >= 2:
                    pv = gidx - 2
                    env.wait(eng, s_aev, ("p0ev", pv // 4, pv % 4))
                ins = None
                for jj in range(4):
                    dt_i = j * 4 + jj
                    ins = eng.transpose(pT[gidx % 2][:, jj * 128:(jj + 1) * 128],
                                        xn_st[tt % 2][:, dt_i * 128:(dt_i + 1) * 128],
                                        ident_t[:, :])
                env.inc(ins, s_ptr, 1, mark=("p0tr", tt, j))
        env.wait(eng, s_va, ("p0bcv", 1))
        env.wait(eng, s_aev, ("p0ev", NTT - 1, 3))
        for h in range(NH):
            env.wait(eng, s_wq[h % 2], ("p0wq", h))
            for qh in range(2):
                g = h * 2 + qh
                if g >= 2:
                    pv = g - 2
                    env.wait(eng, s_apr, ("p0q", pv // 2, pv % 2))
                ins = None
                for dt_i in range(16):
                    ins = eng.matmul(
                        pP[g % 2][:, :],
                        wqh[h % 2][:, dt_i * 128:(dt_i + 1) * 128],
                        xnT_own[:, dt_i * NTOK + qh * 512: dt_i * NTOK + (qh + 1) * 512],
                        start=(dt_i == 0), stop=(dt_i == 15))
                env.inc(ins, s_pq, 1, mark=("p0qm", h, qh))
        for qh in range(2):
            for half in range(2):
                if qh == 1:
                    env.wait(eng, s_apr, ("p0qr", 0, half))
                ins = None
                for dt_i in range(16):
                    ins = eng.matmul(
                        pQR[half][:, :],
                        wqr_b[:, dt_i * DR + half * 32: dt_i * DR + half * 32 + 32],
                        xnT_own[:, dt_i * NTOK + qh * 512: dt_i * NTOK + (qh + 1) * 512],
                        start=(dt_i == 0), stop=(dt_i == 15))
                env.inc(ins, s_pq, 1, mark=("p0qrm", qh, half))

    def p0_scalar(env, eng):
        env.wait(eng, s_ms, "consts")
        for tt in range(NTT):
            env.wait(eng, s_xo[tt % 2], ("p0x", tt))
            if tt >= 2:
                env.wait(eng, s_va, ("p0var", tt - 2))
            env.inc(eng.activation(out=sq_scr[:, :], in_=xo[tt % 2][:, :],
                                   func=AF.Square, accum_out=sqa0[tt % 2][:, :]),
                    s_asq, 1, mark=("p0sq", tt))
            env.wait(eng, s_va, ("p0var", tt))
            if tt >= 2:
                env.wait(eng, s_vt, ("p0rec", tt - 2))
            env.inc(eng.activation(out=std0[tt % 2][:, :], in_=vt0[:, :],
                                   func=AF.Sqrt, bias=epst[:, :]),
                    s_astd, 1, mark=("p0std", tt))
            for j in range(4):
                env.wait(eng, s_ptr, ("p0tr", tt, j))
                dst = xnT_own[:, :].rearrange("p (a n t) -> p a n t", a=16, n=NTT)
                env.inc(eng.copy(out=dst[:, j * 4:(j + 1) * 4, tt, :],
                                 in_=pT[(tt * 4 + j) % 2][:, :].rearrange(
                                     "p (a t) -> p a t", a=4)),
                        s_aev, 1, mark=("p0ev", tt, j))
        for h in range(NH):
            for qh in range(2):
                env.wait(eng, s_pq, ("p0qm", h, qh))
                if h >= 2 and qh == 0:
                    env.wait(eng, s_qsp[h % 2], ("p0qsp", h - 2))
                env.inc(eng.activation(
                    out=qst[h % 2][:, qh * 512:(qh + 1) * 512],
                    in_=pP[(h * 2 + qh) % 2][:, :], func=AF.Copy, scale=SCALE),
                    s_apr, 1, mark=("p0q", h, qh))
        for qh in range(2):
            for half in range(2):
                env.wait(eng, s_pq, ("p0qrm", qh, half))
                dst = qr_lo if half == 0 else qr_hi
                env.inc(eng.activation(out=dst[:, qh * 512:(qh + 1) * 512],
                                       in_=pQR[half][:, :], func=AF.Copy, scale=SCALE),
                        s_apr, 1, mark=("p0qr", qh, half))

    def p0_vector(env, eng):
        for s in range(2):
            env.wait(eng, s_pq, ("p0bc", s))
            env.inc(eng.tensor_copy(out=qpos_bc[:, s * 512:(s + 1) * 512],
                                    in_=pP[s][:, :]),
                    s_va, 1, mark=("p0bcv", s))
        for tt in range(NTT):
            env.wait(eng, s_xo[tt % 2], ("p0x", tt))
            fence(eng, eng.reduce_sum(sums0[tt % 2][:, :], xo[tt % 2][:, :], axis=AX))
            fence(eng, eng.tensor_scalar(out=mt0[:, :], in0=sums0[tt % 2][:, :],
                                         scalar1=INV_D, scalar2=None, op0=ALU.mult))
            env.wait(eng, s_asq, ("p0sq", tt))
            fence(eng, eng.tensor_scalar(out=vt0[:, :], in0=sqa0[tt % 2][:, :],
                                         scalar1=INV_D, scalar2=None, op0=ALU.mult))
            fence(eng, eng.tensor_tensor(out=m20[:, :], in0=mt0[:, :], in1=mt0[:, :],
                                         op=ALU.mult))
            vins = eng.tensor_tensor(out=vt0[:, :], in0=vt0[:, :], in1=m20[:, :],
                                     op=ALU.subtract)
            c = env.inc(vins, s_va, 1, mark=("p0var", tt))
            env.wait_val(eng, s_va, c)
            env.wait(eng, s_astd, ("p0std", tt))
            rins = eng.reciprocal(out=rstd0[:, :], in_=std0[tt % 2][:, :])
            fence(eng, rins)
            env.mark_here(s_vt, ("p0rec", tt))
            if tt >= 2:
                env.wait(eng, s_ptr, ("p0tr", tt - 2, 3))
            env.inc(eng.tensor_scalar(out=xn_st[tt % 2][:, :], in0=xo[tt % 2][:, :],
                                      scalar1=mt0[:, :], scalar2=rstd0[:, :],
                                      op0=ALU.subtract, op1=ALU.mult),
                    s_v, 1, mark=("p0xn", tt))
        env.wait(eng, s_apr, ("p0qr", 1, 1))
        env.wait(eng, s_ld, "consts")
        eng.tensor_tensor(out=rt1[:, :], in0=qr_lo[:, :], in1=cq_t[:, :], op=ALU.mult)
        eng.tensor_tensor(out=rt2[:, :], in0=qr_hi[:, :], in1=sq_t[:, :], op=ALU.mult)
        eng.tensor_tensor(out=qrT_lo[:, :], in0=rt1[:, :], in1=rt2[:, :], op=ALU.subtract)
        eng.tensor_tensor(out=rt1[:, :], in0=qr_hi[:, :], in1=cq_t[:, :], op=ALU.mult)
        eng.tensor_tensor(out=rt2[:, :], in0=qr_lo[:, :], in1=sq_t[:, :], op=ALU.mult)
        env.inc(eng.tensor_tensor(out=qrT_hi[:, :], in0=rt1[:, :], in1=rt2[:, :],
                                  op=ALU.add), s_va, 1, mark="qrT")

    run_phase("p0", {"sync": p0_sync, "gpsimd": p0_gpsimd, "tensor": p0_tensor,
                     "scalar": p0_scalar, "vector": p0_vector})
    ps0.close()
    es0.close()

    es1 = ExitStack()
    sb1 = lambda name, shape, dt: es1.enter_context(nc.sbuf_tensor(name, shape, dt))
    xs = [sb1(f"xs{i}", [128, D], F32) for i in range(6)]
    xn1 = [sb1(f"xn1_{i}", [128, D], BF16) for i in range(6)]
    xnT_g = sb1("xnT_g", [128, 16 * 512], BF16)
    ckvT_g = sb1("ckvT_g", [128, 4 * 512], BF16)
    krr_lo = sb1("krr_lo", [32, 512], F32)
    krr_hi = sb1("krr_hi", [32, 512], F32)
    krb_lo = sb1("krb_lo", [32, 512], BF16)
    krb_hi = sb1("krb_hi", [32, 512], BF16)
    kT_st = sb1("kT_st", [128, 16 * 512], BF16)
    v_st = sb1("v_st", [128, 4 * D], BF16)
    sq1 = sb1("sq1", [128, D], BF16)
    sums1 = [sb1(f"sums1_{i}", [128, 1], F32) for i in range(2)]
    sqa1 = [sb1(f"sqa1_{i}", [128, 1], F32) for i in range(2)]
    m1 = sb1("m1", [128, 1], F32)
    v1 = sb1("v1", [128, 1], F32)
    m21 = sb1("m21", [128, 1], F32)
    std1 = [sb1(f"std1_{i}", [128, 1], F32) for i in range(2)]
    rstd1 = sb1("rstd1", [128, 1], F32)
    rr1 = sb1("rr1", [32, 512], F32)
    rr2 = sb1("rr2", [32, 512], F32)
    ckg = [sb1(f"ckg{i}", [32, 512], F32) for i in range(2)]
    skg = [sb1(f"skg{i}", [32, 512], F32) for i in range(2)]

    ps1 = ExitStack()
    pT1 = [ps1.enter_context(nc.psum_tensor(f"pT1_{i}", [128, 512], BF16)) for i in range(2)]
    pP1 = [ps1.enter_context(nc.psum_tensor(f"pP1_{i}", [128, 512], F32)) for i in range(2)]

    SESS = 38

    def p1_sync(env, eng):
        xfv = x_full.rearrange("(n p) d -> p n d", p=128)
        kTd = kT_s.rearrange("(h p) t -> p h t", p=128)
        vd = v_s.rearrange("(n p) d -> p n d", p=128)

        def loads(g):
            if g >= 2:
                env.wait(eng, s_va, ("p1rope", g - 2))
            env.inc(eng.dma_start(out=ckg[g % 2][:, :],
                                  in_=cosk[:, g * 512:(g + 1) * 512]), s_cs[g % 2], 16)
            env.inc(eng.dma_start(out=skg[g % 2][:, :],
                                  in_=sink[:, g * 512:(g + 1) * 512]),
                    s_cs[g % 2], 16, mark=("p1cs", g))
            for t4 in range(4):
                i = g * 4 + t4
                if i >= 6:
                    env.wait(eng, s_v, ("p1xn", i - 6))
                    env.wait(eng, s_asq, ("p1sq", i - 6))
                env.inc(eng.dma_start(out=xs[i % 6][:, :], in_=xfv[:, i, :]),
                        s_xs[i % 6], 16, mark=("p1x", i))

        def spills(g):
            env.wait(eng, s_apr, ("p1ev", g, 4 + 2 + 16 - 1))
            env.inc(eng.dma_start(
                out=kTd[:, :, g * 512:(g + 1) * 512],
                in_=kT_st[:, :].rearrange("p (h t) -> p h t", h=16)),
                s_spk, 16, mark=("p1spk", g))
            env.wait(eng, s_va, ("p1rope", g))
            env.inc(eng.dma_start(out=krT_s[0:32, g * 512:(g + 1) * 512],
                                  in_=krb_lo[:, :]), s_spr, 16)
            env.inc(eng.dma_start(out=krT_s[32:64, g * 512:(g + 1) * 512],
                                  in_=krb_hi[:, :]), s_spr, 16, mark=("p1spr", g))
            env.wait(eng, s_va, ("p1evv", g, SESS - 1))
            env.inc(eng.dma_start(
                out=vd[:, g * 4:(g + 1) * 4, :],
                in_=v_st[:, :].rearrange("p (n d) -> p n d", n=4)),
                s_spv, 16, mark=("p1spv", g))

        loads(0)
        loads(1)
        for g in range(NGRP):
            if g + 2 < NGRP:
                loads(g + 2)
            spills(g)

    def p1_tensor(env, eng):
        for g in range(NGRP):
            for t4 in range(4):
                i = g * 4 + t4
                env.wait(eng, s_v, ("p1xn", i))
                for j in range(4):
                    gi = i * 4 + j
                    if gi >= 2:
                        pv = gi - 2
                        env.wait(eng, s_aev, ("p1evT", pv // 4, pv % 4))
                    ins = None
                    for jj in range(4):
                        dt_i = j * 4 + jj
                        ins = eng.transpose(pT1[gi % 2][:, jj * 128:(jj + 1) * 128],
                                            xn1[i % 6][:, dt_i * 128:(dt_i + 1) * 128],
                                            ident_t[:, :])
                    env.inc(ins, s_ptr, 1, mark=("p1tr", i, j))
            env.wait(eng, s_aev, ("p1evT", g * 4 + 3, 3))
            if g == 0:
                env.wait(eng, s_w, "attnw")
            sess = [0]

            def psess(emitter, mark_extra=None):
                gs = g * SESS + sess[0]
                if gs >= 2:
                    pv = gs - 2
                    pg, psi = pv // SESS, pv % SESS
                    if psi >= 22:
                        env.wait(eng, s_va, ("p1evv", pg, psi))
                    else:
                        env.wait(eng, s_apr, ("p1ev", pg, psi))
                ins = emitter(pP1[gs % 2])
                env.inc(ins, s_pq, 1, mark=("p1mm", g, sess[0]))
                sess[0] += 1

            for dct in range(4):
                def emit(bank, dct=dct):
                    ins = None
                    for dt_i in range(16):
                        ins = eng.matmul(
                            bank[:, :],
                            wkv_b[:, dt_i * DC + dct * 128: dt_i * DC + (dct + 1) * 128],
                            xnT_g[:, dt_i * 512:(dt_i + 1) * 512],
                            start=(dt_i == 0), stop=(dt_i == 15))
                    return ins
                psess(emit)
            env.wait(eng, s_apr, ("p1ev", g, 3))

            for half in range(2):
                def emit_kr(bank, half=half):
                    ins = None
                    for dct in range(4):
                        ins = eng.matmul(
                            bank[0:32, :],
                            wkr_b[:, dct * DR + half * 32: dct * DR + half * 32 + 32],
                            ckvT_g[:, dct * 512:(dct + 1) * 512],
                            start=(dct == 0), stop=(dct == 3))
                    return ins
                psess(emit_kr)
            for h in range(NH):
                def emit_k(bank, h=h):
                    ins = None
                    for dct in range(4):
                        ins = eng.matmul(
                            bank[:, :],
                            wkup_b[:, dct * D + h * 128: dct * D + (h + 1) * 128],
                            ckvT_g[:, dct * 512:(dct + 1) * 512],
                            start=(dct == 0), stop=(dct == 3))
                    return ins
                psess(emit_k)
            for t4 in range(4):
                for vg in range(4):
                    def emit_v(bank, t4=t4, vg=vg):
                        ins = None
                        for dct in range(4):
                            ins = eng.matmul(
                                bank[:, :],
                                ckvT_g[:, dct * 512 + t4 * 128: dct * 512 + (t4 + 1) * 128],
                                wvup_b[:, dct * D + vg * 512: dct * D + (vg + 1) * 512],
                                start=(dct == 0), stop=(dct == 3))
                        return ins
                    psess(emit_v)

    def p1_scalar(env, eng):
        def sq_tiles(g):
            for t4 in range(4):
                i = g * 4 + t4
                env.wait(eng, s_xs[i % 6], ("p1x", i))
                if i >= 2:
                    env.wait(eng, s_va, ("p1var", i - 2))
                env.inc(eng.activation(out=sq1[:, :], in_=xs[i % 6][:, :],
                                       func=AF.Square, accum_out=sqa1[i % 2][:, :]),
                        s_asq, 1, mark=("p1sq", i))
                env.wait(eng, s_va, ("p1var", i))
                if i >= 2:
                    env.wait(eng, s_vt, ("p1rec", i - 2))
                env.inc(eng.activation(out=std1[i % 2][:, :], in_=v1[:, :],
                                       func=AF.Sqrt, bias=epst[:, :]),
                        s_astd, 1, mark=("p1std", i))

        def sq_tile(g, t4):
            i = g * 4 + t4
            env.wait(eng, s_xs[i % 6], ("p1x", i))
            if i >= 2:
                env.wait(eng, s_va, ("p1var", i - 2))
            env.inc(eng.activation(out=sq1[:, :], in_=xs[i % 6][:, :],
                                   func=AF.Square, accum_out=sqa1[i % 2][:, :]),
                    s_asq, 1, mark=("p1sq", i))
            env.wait(eng, s_va, ("p1var", i))
            if i >= 2:
                env.wait(eng, s_vt, ("p1rec", i - 2))
            env.inc(eng.activation(out=std1[i % 2][:, :], in_=v1[:, :],
                                   func=AF.Sqrt, bias=epst[:, :]),
                    s_astd, 1, mark=("p1std", i))

        def evacs_xnT_tile(g, t4):
            if True:
                i = g * 4 + t4
                for j in range(4):
                    gi = i * 4 + j
                    env.wait(eng, s_ptr, ("p1tr", i, j))
                    if g >= 1 and t4 == 0 and j == 0:
                        env.wait(eng, s_pq, ("p1mm", g - 1, SESS - 1))
                    dst = xnT_g[:, :].rearrange("p (a n t) -> p a n t", a=16, n=4)
                    env.inc(eng.copy(out=dst[:, j * 4:(j + 1) * 4, t4, :],
                                     in_=pT1[gi % 2][:, :].rearrange(
                                         "p (a t) -> p a t", a=4)),
                            s_aev, 1, mark=("p1evT", i, j))

        def pevacs(g):
            sess = [0]

            def pevac(dst_ap, part32=False):
                si = sess[0]
                env.wait(eng, s_pq, ("p1mm", g, si))
                bank = pP1[(g * SESS + si) % 2]
                srcb = bank[0:32, :] if part32 else bank[:, :]
                env.inc(eng.copy(out=dst_ap, in_=srcb), s_apr, 1, mark=("p1ev", g, si))
                sess[0] += 1

            if g >= 1:
                env.wait(eng, s_spk, ("p1spk", g - 1))
                env.wait(eng, s_spr, ("p1spr", g - 1))
            for dct in range(4):
                pevac(ckvT_g[:, dct * 512:(dct + 1) * 512])
            pevac(krr_lo[:, :], part32=True)
            pevac(krr_hi[:, :], part32=True)
            for h in range(NH):
                pevac(kT_st[:, h * 512:(h + 1) * 512])

        sq_tiles(0)
        for g in range(NGRP):
            for t4 in range(4):
                if g + 1 < NGRP:
                    sq_tile(g + 1, t4)
                evacs_xnT_tile(g, t4)
            pevacs(g)

    def p1_vector(env, eng):
        def ln_tiles(g):
            for t4 in range(4):
                i = g * 4 + t4
                env.wait(eng, s_xs[i % 6], ("p1x", i))
                fence(eng, eng.reduce_sum(sums1[i % 2][:, :], xs[i % 6][:, :], axis=AX))
                fence(eng, eng.tensor_scalar(out=m1[:, :], in0=sums1[i % 2][:, :],
                                             scalar1=INV_D, scalar2=None, op0=ALU.mult))
                env.wait(eng, s_asq, ("p1sq", i))
                fence(eng, eng.tensor_scalar(out=v1[:, :], in0=sqa1[i % 2][:, :],
                                             scalar1=INV_D, scalar2=None, op0=ALU.mult))
                fence(eng, eng.tensor_tensor(out=m21[:, :], in0=m1[:, :], in1=m1[:, :],
                                             op=ALU.mult))
                vins = eng.tensor_tensor(out=v1[:, :], in0=v1[:, :], in1=m21[:, :],
                                         op=ALU.subtract)
                c = env.inc(vins, s_va, 1, mark=("p1var", i))
                env.wait_val(eng, s_va, c)
                env.wait(eng, s_astd, ("p1std", i))
                rins = eng.reciprocal(out=rstd1[:, :], in_=std1[i % 2][:, :])
                fence(eng, rins)
                env.mark_here(s_vt, ("p1rec", i))
                if i >= 6:
                    env.wait(eng, s_ptr, ("p1tr", i - 6, 3))
                env.inc(eng.tensor_scalar(out=xn1[i % 6][:, :], in0=xs[i % 6][:, :],
                                          scalar1=m1[:, :], scalar2=rstd1[:, :],
                                          op0=ALU.subtract, op1=ALU.mult),
                        s_v, 1, mark=("p1xn", i))

        def tail(g):
            env.wait(eng, s_apr, ("p1ev", g, 5))
            env.wait(eng, s_cs[g % 2], ("p1cs", g))
            if g >= 1:
                env.wait(eng, s_spr, ("p1spr", g - 1))
            ck_g, sk_g = ckg[g % 2], skg[g % 2]
            eng.tensor_tensor(out=rr1[:, :], in0=krr_lo[:, :], in1=ck_g[:, :], op=ALU.mult)
            eng.tensor_tensor(out=rr2[:, :], in0=krr_hi[:, :], in1=sk_g[:, :], op=ALU.mult)
            eng.tensor_tensor(out=krb_lo[:, :], in0=rr1[:, :], in1=rr2[:, :], op=ALU.subtract)
            eng.tensor_tensor(out=rr1[:, :], in0=krr_hi[:, :], in1=ck_g[:, :], op=ALU.mult)
            eng.tensor_tensor(out=rr2[:, :], in0=krr_lo[:, :], in1=sk_g[:, :], op=ALU.mult)
            env.inc(eng.tensor_tensor(out=krb_hi[:, :], in0=rr1[:, :], in1=rr2[:, :],
                                      op=ALU.add), s_va, 1, mark=("p1rope", g))
            if g >= 1:
                env.wait(eng, s_spv, ("p1spv", g - 1))
            for vi in range(16):
                si = 22 + vi
                t4, vg = divmod(vi, 4)
                env.wait(eng, s_pq, ("p1mm", g, si))
                env.inc(eng.tensor_copy(
                    out=v_st[:, t4 * D + vg * 512: t4 * D + (vg + 1) * 512],
                    in_=pP1[(g * SESS + si) % 2][:, :]),
                    s_va, 1, mark=("p1evv", g, si))

        ln_tiles(0)
        for g in range(NGRP):
            if g + 1 < NGRP:
                ln_tiles(g + 1)
            tail(g)

    run_phase("p1", {"sync": p1_sync, "tensor": p1_tensor,
                     "scalar": p1_scalar, "vector": p1_vector})
    ps1.close()
    es1.close()
    esW.close()

    es2 = ExitStack()
    sb2 = lambda name, shape, dt: es2.enter_context(nc.sbuf_tensor(name, shape, dt))
    qT_sl = [sb2(f"qT_sl{i}", [128, NH * 512], BF16) for i in range(2)]
    ktw = [sb2(f"ktw{i}", [128, 8 * WIN * 128], BF16) for i in range(2)]
    vw = [sb2(f"vw{i}", [128, WIN * 8 * 128], BF16) for i in range(2)]
    krw_lo = [sb2(f"krw_lo{i}", [32, WIN * 128], BF16) for i in range(2)]
    krw_hi = [sb2(f"krw_hi{i}", [32, WIN * 128], BF16) for i in range(2)]
    av_sb = sb2("av_sb", [128, NH * 512], F32)
    rs_sb = sb2("rs_sb", [1, NH * 512], F32)
    rsp_t = sb2("rsp_t", [128, 64], F32)
    earR = [sb2("earR0", [128, 512], F32)]
    earM = sb2("earM", [128, WIN * 512], BF16)
    p_buf = [sb2(f"p_buf{i}", [128, 1024], BF16) for i in range(7)]
    pr_buf = [sb2(f"pr_buf{i}", [128, 1024], BF16) for i in range(3)]


    ps2 = ExitStack()
    pac2 = [ps2.enter_context(nc.psum_tensor(f"pac2_{i}", [128, 1024], F32)) for i in range(2)]
    pac = [pac2[0][:, 0:512], pac2[0][:, 512:1024],
           pac2[1][:, 0:512], pac2[1][:, 512:1024]]
    pav = [ps2.enter_context(nc.psum_tensor(f"pav{i}", [128, 512], F32)) for i in range(2)]
    prs = [ps2.enter_context(nc.psum_tensor(f"prs{i}", [1, 512], F32)) for i in range(2)]

    NW = (SLOT_KTS[0] // WIN, SLOT_KTS[1] // WIN)

    def gw_of(sl, w):
        return (NW[0] if sl else 0) + w

    def p2_sync(env, eng):
        kTd = kT_s.rearrange("(h p) t -> p h t", p=128)
        vd = v_s.rearrange("(n p) d -> p n d", p=128)
        qd = qT_s.rearrange("p (h t) -> p h t", h=NH)
        avd = avn_s.rearrange("p (n c) -> p n c", n=NTT)

        def qload(sl):
            env.inc(eng.dma_start(out=qT_sl[sl][:, :].rearrange("p (h t) -> p h t", h=NH),
                                  in_=qd[:, :, sl * 512:(sl + 1) * 512]),
                    s_qsl, 16, mark=("p2q", sl))

        def windows(sl, ws):
            for w in ws:
                gw = gw_of(sl, w)
                kt0 = w * WIN
                if gw == 0:
                    env.wait(eng, s_spk, ("p1spk", NGRP - 1))
                    env.wait(eng, s_spr, ("p1spr", NGRP - 1))
                    env.wait(eng, s_spv, ("p1spv", NGRP - 1))
                with env.cond(eng, ("wsk", gw, "sy"), win_fl[0:1, gw:gw + 1]):
                    if gw >= 1:
                        env.wait(eng, s_phw, ("p2hw", gw - 1, NH - 1))
                    env.inc(eng.dma_start(out=krw_lo[gw % 2][:, :],
                                          in_=krT_s[0:32, kt0 * 128:(kt0 + WIN) * 128]),
                            s_krw[gw % 2], 16)
                    env.inc(eng.dma_start(out=krw_hi[gw % 2][:, :],
                                          in_=krT_s[32:64, kt0 * 128:(kt0 + WIN) * 128]),
                            s_krw[gw % 2], 16, mark=("p2kv", gw))
                    for hf in range(2):
                        hh = gw * 2 + hf
                        if hh >= 2:
                            pw, pf = (hh - 2) // 2, (hh - 2) % 2
                            env.wait(eng, s_phw, ("p2hw", pw, pf * 8 + 7))
                        env.inc(eng.dma_start(
                            out=ktw[hh % 2][:, :].rearrange("p (h t) -> p h t", h=8),
                            in_=kTd[:, hf * 8:(hf + 1) * 8, kt0 * 128:(kt0 + WIN) * 128]),
                            s_kvh[hh % 2], 16)
                        env.inc(eng.dma_start(
                            out=vw[hh % 2][:, :].rearrange("p (n c) -> p n c", n=WIN),
                            in_=vd[:, kt0:kt0 + WIN, hf * 1024:(hf + 1) * 1024]),
                            s_kvh[hh % 2], 16, mark=("p2kvh", gw, hf))

        def dance(sl):
            env.wait(eng, s_vrs, ("p2rs", gw_of(sl, NW[sl] - 1) * NH + NH - 1))
            c = env.inc(eng.dma_start(out=rs_d[sl:sl + 1, :], in_=rs_sb[:, :]),
                        s_rsp, 16)
            env.wait_val(eng, s_rsp, c)
            env.inc(eng.dma_start(
                out=rsp_t[:, :],
                in_=rs_d[sl:sl + 1, :].rearrange("o (p c) -> (o p) c", p=128)),
                s_rsp, 16, mark=("p2rsp", sl))
            env.wait(eng, s_vav, ("p2av", gw_of(sl, NW[sl] - 1) * NH + NH - 1))
            env.inc(eng.dma_start(out=av_s[sl * 128:(sl + 1) * 128, :],
                                  in_=av_sb[:, :]),
                    s_avnsp[sl], 16, mark=("p2avsp", sl))
            env.wait(eng, s_va, ("p2rcp", sl))
            env.inc(eng.dma_start(
                out=rs_d[sl:sl + 1, :].rearrange("o (p c) -> (o p) c", p=128),
                in_=rsp_t[:, :]), s_rsp, 16, mark=("p2rrb", sl))

        env.wait(eng, s_qsp[0], ("p0qsp", NH - 2))
        env.wait(eng, s_qsp[1], ("p0qsp", NH - 1))
        qload(0)
        qload(1)
        windows(0, range(NW[0]))
        windows(1, range(1))
        dance(0)
        windows(1, range(1, NW[1]))
        dance(1)

    def p2_tensor(env, eng):
        L = 6
        env.wait(eng, s_va, "qrT")
        for sl in range(2):
            kbase = SLOT_KTS[0] if sl else 0
            env.wait(eng, s_qsl, ("p2q", 1))
            for w in range(NW[sl]):
                gw = gw_of(sl, w)
                with env.cond(eng, ("wsk", gw, "te"), win_fl[0:1, gw:gw + 1]):
                 env.wait(eng, s_krw[gw % 2], ("p2kv", gw))
                 for kt in range(WIN):
                    gkt = kbase + w * WIN + kt
                    if kt < 4 and gw >= 1:
                        env.wait(eng, s_apr, ("p2pr", (gw * 128 - 4 + kt) | 1))
                    if kt >= 4:
                        env.wait(eng, s_aear, ("p2ear", gkt - 4))
                    eng.matmul(pac[kt % 4],
                               krw_lo[gw % 2][:, kt * 128:(kt + 1) * 128],
                               qrT_lo[:, sl * 512:(sl + 1) * 512],
                               start=True, stop=False)
                    env.inc(eng.matmul(pac[kt % 4],
                                       krw_hi[gw % 2][:, kt * 128:(kt + 1) * 128],
                                       qrT_hi[:, sl * 512:(sl + 1) * 512],
                                       start=False, stop=True),
                            s_ar, 1, mark=("p2ar", gkt))
                 for i in range(128 + L):
                    if i < 128:
                        h, kt = divmod(i, WIN)
                        gac = gw * 128 + i
                        hh = gw * 2 + h // 8
                        if i % 64 == 0:
                            env.wait(eng, s_kvh[hh % 2], ("p2kvh", gw, h // 8))
                        if i < 4:
                            env.wait(eng, s_aear, ("p2ear", kbase + w * WIN + 4 + i))
                        elif i % 2 == 0:
                            env.wait(eng, s_apr, ("p2pr", (gac - 4) | 1))
                        ins2 = eng.matmul(
                            pac[gac % 4],
                            ktw[hh % 2][:, (h % 8) * WIN * 128 + kt * 128: (h % 8) * WIN * 128 + (kt + 1) * 128],
                            qT_sl[sl][:, h * 512:(h + 1) * 512],
                            start=True, stop=True)
                        if i % 2 == 1 or i == 127:
                            env.inc(ins2, s_pac, 1, mark=("p2ac", gac))
                        else:
                            env.mark_here(s_pac, ("p2ac", gac))
                    if i >= L:
                        j = i - L
                        hj, ktj = divmod(j, WIN)
                        gacj = gw * 128 + j
                        ghwj = gw * NH + hj
                        hhj = gw * 2 + hj // 8
                        if ktj == 0 and ghwj >= 2:
                            env.wait(eng, s_vav, ("p2av", ghwj - 2))
                            env.wait(eng, s_vrs, ("p2rs", ghwj - 2))
                        if j % 2 == 0:
                            env.wait(eng, s_vp, ("p2p", (gacj + 1)))
                        pslice = p_buf[(gacj // 2) % 7][:, (gacj % 2) * 512:(gacj % 2) * 512 + 512]
                        eng.matmul(pav[hj % 2][:, :],
                                   vw[hhj % 2][:, ktj * 1024 + (hj % 8) * 128: ktj * 1024 + (hj % 8) * 128 + 128],
                                   pslice,
                                   start=(ktj == 0), stop=(ktj == WIN - 1))
                        if ktj == WIN - 1:
                            for kt2 in range(WIN):
                                g2 = gw * 128 + hj * WIN + kt2
                                ps2l = p_buf[(g2 // 2) % 7][:, (g2 % 2) * 512:(g2 % 2) * 512 + 512]
                                ins = eng.matmul(prs[hj % 2][:, :], ones_b[:, :],
                                                 ps2l,
                                                 start=(kt2 == 0), stop=(kt2 == WIN - 1))
                            env.inc(ins, s_phw, 1, mark=("p2hw", gw, hj))

    def p2_scalar(env, eng):
        for sl in range(2):
            kbase = SLOT_KTS[0] if sl else 0
            for w in range(NW[sl]):
                gw = gw_of(sl, w)
                with env.cond(eng, ("wsk", gw, "ac"), win_fl[0:1, gw:gw + 1]):
                 for kt in range(WIN):
                    gkt = kbase + w * WIN + kt
                    env.wait(eng, s_ar, ("p2ar", gkt))
                    if gkt >= 1:
                        env.wait(eng, s_vear, ("p2em", gkt - 1))
                    env.inc(eng.activation(out=earR[0][:, :],
                                           in_=pac[kt % 4], func=AF.Exp),
                            s_aear, 1, mark=("p2ear", gkt))
                 for i in range(0, 128, 2):
                    gac = gw * 128 + i
                    env.wait(eng, s_pac, ("p2ac", gac + 1))
                    if gac >= 6:
                        env.wait(eng, s_vp, ("p2p", gac - 5))
                    env.inc(eng.activation(out=pr_buf[(gac // 2) % 3][:, :],
                                           in_=pac2[(gac // 2) % 2][:, :], func=AF.Exp),
                            s_apr, 1, mark=("p2pr", gac + 1))

    def p2_vector(env, eng):
        for sl in range(2):
            kbase = SLOT_KTS[0] if sl else 0
            for w in range(NW[sl]):
                gw = gw_of(sl, w)
                with env.cond(eng, ("wsk", gw, "ve"), win_fl[0:1, gw:gw + 1]):
                 for kt in range(WIN):
                    gkt = kbase + w * WIN + kt
                    eng.tensor_scalar(
                        out=earM[:, kt * 512:(kt + 1) * 512],
                        in0=qpos_bc[:, sl * 512:(sl + 1) * 512],
                        scalar1=float((w * WIN + kt) * 128), scalar2=iota_t[:, :],
                        op0=ALU.subtract, op1=ALU.is_ge)
                    env.wait(eng, s_aear, ("p2ear", gkt))
                    env.inc(eng.tensor_tensor(out=earM[:, kt * 512:(kt + 1) * 512],
                                              in0=earM[:, kt * 512:(kt + 1) * 512],
                                              in1=earR[0][:, :], op=ALU.mult),
                            s_vear, 1, mark=("p2em", gkt))
                 for h in range(NH):
                    ghw = gw * NH + h
                    for kt in range(0, WIN, 2):
                        gac = ghw * WIN + kt
                        env.wait(eng, s_apr, ("p2pr", gac + 1))
                        env.inc(eng.tensor_tensor(out=p_buf[(gac // 2) % 7][:, :],
                                                  in0=pr_buf[(gac // 2) % 3][:, :],
                                                  in1=earM[:, kt * 512:(kt + 2) * 512],
                                                  op=ALU.mult),
                                s_vp, 2, mark=("p2p", gac + 1)),
                        env.mark_here(s_vp, ("p2p", gac))
                    env.wait(eng, s_phw, ("p2hw", gw, h))
                    if w == 0:
                        if sl == 1:
                            env.wait(eng, s_avnsp[0], ("p2avsp", 0))
                            if h == 0:
                                env.wait(eng, s_rsp, ("p2rsp", 0))
                        env.inc(eng.tensor_copy(out=av_sb[:, h * 512:(h + 1) * 512],
                                                in_=pav[h % 2][:, :]),
                                s_vav, 1, mark=("p2av", ghw))
                        env.inc(eng.tensor_copy(out=rs_sb[:, h * 512:(h + 1) * 512],
                                                in_=prs[h % 2][0:1, :]),
                                s_vrs, 1, mark=("p2rs", ghw))
                    else:
                        env.inc(eng.tensor_tensor(out=av_sb[:, h * 512:(h + 1) * 512],
                                                  in0=av_sb[:, h * 512:(h + 1) * 512],
                                                  in1=pav[h % 2][:, :], op=ALU.add),
                                s_vav, 1, mark=("p2av", ghw))
                        env.inc(eng.tensor_tensor(out=rs_sb[:, h * 512:(h + 1) * 512],
                                                  in0=rs_sb[:, h * 512:(h + 1) * 512],
                                                  in1=prs[h % 2][0:1, :], op=ALU.add),
                                s_vrs, 1, mark=("p2rs", ghw))
                if sl == 1 and w == 0:
                    env.wait(eng, s_rsp, ("p2rsp", 0))
                    rins = eng.reciprocal(out=rsp_t[:, :], in_=rsp_t[:, :])
                    c = env.inc(rins, s_va, 1, mark=("p2rcp", 0))
                    env.wait_val(eng, s_va, c)


    def p2_vector_tail(env, eng):
        env.wait(eng, s_rsp, ("p2rsp", 1))
        rins = eng.reciprocal(out=rsp_t[:, :], in_=rsp_t[:, :])
        c = env.inc(rins, s_va, 1, mark=("p2rcp", 1))
        env.wait_val(eng, s_va, c)

    _p2_vec = p2_vector

    def p2_vector(env, eng):
        _p2_vec(env, eng)
        p2_vector_tail(env, eng)

    run_phase("p2", {"sync": p2_sync, "tensor": p2_tensor,
                     "scalar": p2_scalar, "vector": p2_vector})
    ps2.close()
    es2.close()
    esQ.close()

    esDA = ExitStack()
    down_acc = esDA.enter_context(nc.sbuf_tensor("down_acc", [128, NTT * D], F32))

    es25 = ExitStack()
    sb25 = lambda name, shape, dt: es25.enter_context(nc.sbuf_tensor(name, shape, dt))
    wo_b = [sb25(f"wo_b{i}", [128, 16 * 512], BF16) for i in range(4)]
    av_tt = [sb25(f"av_tt{i}", [128, D], F32) for i in range(2)]
    avn_tt = [sb25(f"avn_tt{i}", [128, D], BF16) for i in range(2)]
    rr_tt = [sb25(f"rr_tt{i}", [1, D], F32) for i in range(2)]
    xo25 = [sb25(f"xo25_{i}", [128, 512], F32) for i in range(2)]

    ps25 = ExitStack()
    pW = [ps25.enter_context(nc.psum_tensor(f"pW{i}", [128, 512], F32)) for i in range(2)]
    pB = [ps25.enter_context(nc.psum_tensor(f"pB{i}", [128, 512], F32)) for i in range(2)]

    def p25_sync(env, eng):
        xov = x_own.rearrange("(n p) d -> p n d", p=128)
        for tt in range(NTT):
            sl, tl = tt // 4, tt % 4
            if tt >= 2:
                env.wait(eng, s_va, ("p25n", tt - 2, 3))
            env.wait(eng, s_avnsp[sl], ("p2avsp", sl))
            env.inc(eng.dma_start(
                out=av_tt[tt % 2][:, :].rearrange("p (h c) -> p h c", h=NH),
                in_=av_s.rearrange("(s p) (h q) -> s p h q", s=2, h=NH)[
                    sl, :, :, tl * 128:(tl + 1) * 128]),
                s_a25[tt % 2], 16)
            env.wait(eng, s_rsp, ("p2rrb", sl))
            env.inc(eng.dma_start(
                out=rr_tt[tt % 2][:, :].rearrange("o (h c) -> o h c", h=NH),
                in_=rs_d.rearrange("s (h q) -> s h q", h=NH)[
                    sl, :, tl * 128:(tl + 1) * 128].unsqueeze(0)),
                s_a25[tt % 2], 16, mark=("p25l", tt))
            for dg in range(4):
                i = tt * 4 + dg
                if i >= 2:
                    env.wait(eng, s_vd, ("p25", i - 2))
                env.inc(eng.dma_start(out=xo25[i % 2][:, :],
                                      in_=xov[:, tt, dg * 512:(dg + 1) * 512]),
                        s_xo[i % 2], 16, mark=("p25x", i))

    def p25_gpsimd(env, eng):
        wov = Wo.rearrange("(a p) d -> p a d", p=128)
        for dg in range(4):
            if dg >= 2:
                env.wait(eng, s_wo[dg % 2], ("p25w", dg - 2))
            env.inc(eng.dma_start(
                out=wo_b[dg][:, :].rearrange("p (a c) -> p a c", a=16),
                in_=wov[:, :, dg * 512:(dg + 1) * 512]),
                s_wo[dg % 2], 16, mark=("p25w", dg))

    def p25_tensor(env, eng):
        for tt in range(NTT):
            env.wait(eng, s_a25[tt % 2], ("p25l", tt))
            for quad in range(4):
                if tt * 4 + quad >= 2:
                    env.wait(eng, s_va, ("p25n", *divmod(tt * 4 + quad - 2, 4)))
                env.inc(eng.matmul(pB[quad % 2][:, :], ones_f1[:, :],
                                   rr_tt[tt % 2][:, quad * 512:(quad + 1) * 512],
                                   start=True, stop=True),
                        s_pbc, 1, mark=("p25b", tt, quad))
            for dg in range(4):
                i = tt * 4 + dg
                env.wait(eng, s_wo[dg % 2], ("p25w", dg))
                env.wait(eng, s_va, ("p25n", tt, 3))
                if i >= 2:
                    env.wait(eng, s_vd, ("p25", i - 2))
                ins = None
                for ft in range(16):
                    ins = eng.matmul(pW[i % 2][:, :],
                                     avn_tt[tt % 2][:, ft * 128:(ft + 1) * 128],
                                     wo_b[dg][:, ft * 512:(ft + 1) * 512],
                                     start=(ft == 0), stop=(ft == 15))
                env.inc(ins, s_pac, 1, mark=("p25m", i))

    def p25_vector(env, eng):
        for tt in range(NTT):
            if tt >= 2:
                env.wait(eng, s_pac, ("p25m", (tt - 2) * 4 + 3))
            for quad in range(4):
                env.wait(eng, s_pbc, ("p25b", tt, quad))
                env.inc(eng.tensor_tensor(
                    out=avn_tt[tt % 2][:, quad * 512:(quad + 1) * 512],
                    in0=av_tt[tt % 2][:, quad * 512:(quad + 1) * 512],
                    in1=pB[quad % 2][:, :], op=ALU.mult),
                    s_va, 1, mark=("p25n", tt, quad))
            for dg in range(4):
                i = tt * 4 + dg
                env.wait(eng, s_pac, ("p25m", i))
                env.wait(eng, s_xo[i % 2], ("p25x", i))
                env.inc(eng.tensor_tensor(
                    out=down_acc[:, tt * D + dg * 512: tt * D + (dg + 1) * 512],
                    in0=pW[i % 2][:, :], in1=xo25[i % 2][:, :], op=ALU.add),
                    s_vd, 1, mark=("p25", i))

    run_phase("p25", {"sync": p25_sync, "gpsimd": p25_gpsimd,
                      "tensor": p25_tensor, "vector": p25_vector})
    ps25.close()
    es25.close()

    es3 = ExitStack()
    sb3 = lambda name, shape, dt: es3.enter_context(nc.sbuf_tensor(name, shape, dt))
    xnT3 = sb3("xnT3", [128, 16 * NTOK], BF16)
    h_gb = [sb3(f"h_gb{i}", [128, 1024], F32) for i in range(NTT)]
    hs_st = [sb3(f"hs_st{i}", [128, 1024], BF16) for i in range(2)]
    hsT_gb = sb3("hsT_gb", [128, 8 * NTOK], BF16)
    wgub = [sb3(f"wgub{i}", [128, 16 * 512], BF16) for i in range(2)]
    wdb = [sb3(f"wdb{i}", [128, 8 * 512], BF16) for i in range(2)]
    sg_t = [sb3(f"sg_t{i}", [128, 256], F32) for i in range(2)]
    A_t = [sb3(f"A_t{i}", [128, 256], F32) for i in range(4)]
    wc_t = [sb3(f"wc_t{i}", [128, 256], BF16) for i in range(2)]
    ss_t = sb3("ss_t", [128, 256], BF16)

    ps3 = ExitStack()
    pGU = [ps3.enter_context(nc.psum_tensor(f"pGU{i}", [128, 512], F32)) for i in range(2)]
    pT3 = [ps3.enter_context(nc.psum_tensor(f"pT3_{i}", [128, 512], BF16)) for i in range(2)]
    pD = [ps3.enter_context(nc.psum_tensor(f"pD{i}", [128, 512], F32)) for i in range(2)]

    def p3_sync(env, eng):
        env.wait(eng, s_xnt, "xnT")
        env.inc(eng.dma_start(out=xnT3[:, :], in_=xnT_s), s_xnt3, 16, mark="p3xnT")

    def p3_gpsimd(env, eng):
        wgv = wg_p.rearrange("(a p) i -> p a i", p=128)
        wuv = wu_p.rearrange("(a p) i -> p a i", p=128)
        wdv = wd_p.rearrange("(ph gb s p) d -> p ph gb s d", ph=4, gb=NGB, s=2, p=128)
        for gb in range(NGB):
            for ph in range(4):
                i = gb * 4 + ph
                c0 = ph * 2048 + gb * 256
                if i >= 2:
                    env.wait(eng, s_pac, ("p3u", i - 2, NTT - 1))
                wv = wgub[i % 2][:, :].rearrange("p (a h c) -> p a h c", a=16, h=2)
                env.inc(eng.dma_start(out=wv[:, :, 0, :],
                                      in_=wgv[:, :, c0:c0 + 256]), s_w3[i % 2], 16)
                env.inc(eng.dma_start(out=wv[:, :, 1, :],
                                      in_=wuv[:, :, c0:c0 + 256]), s_w3[i % 2], 16,
                        mark=("p3w", i))
            for dg in range(4):
                i = gb * 4 + dg
                if dg < 2:
                    if gb >= 1:
                        env.wait(eng, s_pac, ("p3d", gb - 1, 31))
                else:
                    env.wait(eng, s_pac, ("p3d", gb, 15))
                for ph in range(4):
                    env.inc(eng.dma_start(
                        out=wdb[i % 2][:, ph * 1024:(ph + 1) * 1024].rearrange(
                            "p (a c) -> p a c", a=2),
                        in_=wdv[:, ph, gb, :, dg * 512:(dg + 1) * 512]),
                        s_wd, 16, mark=("p3wd", i) if ph == 3 else None)

    def p3_tensor(env, eng):
        env.wait(eng, s_xnt3, "p3xnT")
        for gb in range(NGB):
            for ph in range(4):
                i = gb * 4 + ph
                env.wait(eng, s_w3[i % 2], ("p3w", i))
                for tt in range(NTT):
                    gidx = i * NTT + tt
                    if gidx >= 2:
                        env.wait(eng, s_asl, ("p3sl", gidx - 2))
                        env.wait(eng, s_vh, ("p3h", gidx - 2))
                    ins = None
                    for dt_i in range(16):
                        ins = eng.matmul(
                            pGU[gidx % 2][:, :],
                            xnT3[:, dt_i * NTOK + tt * 128: dt_i * NTOK + (tt + 1) * 128],
                            wgub[i % 2][:, dt_i * 512:(dt_i + 1) * 512],
                            start=(dt_i == 0), stop=(dt_i == 15))
                    env.inc(ins, s_pac, 1, mark=("p3u", i, tt))
            def down_sess(ds, dg, tt):
                i = gb * 4 + dg
                env.wait(eng, s_wd, ("p3wd", i))
                gds = gb * 32 + ds
                if gds >= 2:
                    env.wait(eng, s_vd, ("p3dv", gds - 2))
                ins = None
                for it in range(8):
                    ins = eng.matmul(
                        pD[gds % 2][:, :],
                        hsT_gb[:, it * NTOK + tt * 128: it * NTOK + (tt + 1) * 128],
                        wdb[i % 2][:, it * 512:(it + 1) * 512],
                        start=(it == 0), stop=(it == 7))
                env.inc(ins, s_pac, 1, mark=("p3d", gb, ds))

            for tt in range(NTT):
                env.wait(eng, s_vhs, ("p3hs", gb, tt))
                for j in range(2):
                    gi = (gb * NTT + tt) * 2 + j
                    if gi >= 2:
                        pv = gi - 2
                        env.wait(eng, s_aev, ("p3evT", pv // 2, pv % 2))
                    ins = None
                    for jj in range(4):
                        it = j * 4 + jj
                        ins = eng.transpose(pT3[gi % 2][:, jj * 128:(jj + 1) * 128],
                                            hs_st[tt % 2][:, it * 128:(it + 1) * 128],
                                            ident_t[:, :])
                    env.inc(ins, s_ptr, 1, mark=("p3tr", gb * NTT + tt, j))
                env.wait(eng, s_aev, ("p3evT", gb * NTT + tt, 1))
                for dgl in range(2):
                    down_sess(tt * 2 + dgl, dgl, tt)
            for dg in range(2, 4):
                for tt in range(NTT):
                    down_sess(16 + (dg - 2) * 8 + tt, dg, tt)

    def p3_scalar(env, eng):
        for gb in range(NGB):
            for ph in range(4):
                i = gb * 4 + ph
                for tt in range(NTT):
                    gidx = i * NTT + tt
                    env.wait(eng, s_pac, ("p3u", i, tt))
                    if gidx >= 2:
                        env.wait(eng, s_vh, ("p3h", gidx - 2))
                    env.inc(eng.activation(out=sg_t[gidx % 2][:, :],
                                           in_=pGU[gidx % 2][:, 0:256], func=AF.Silu),
                            s_asl, 1, mark=("p3sl", gidx))

            def squares(tt):
                env.wait(eng, s_vh, ("p3h", (gb * 4 + 3) * NTT + tt))
                if tt >= 1:
                    env.wait(eng, s_vhs, ("p3hs", gb, tt - 1))
                ins = None
                for pp in range(4):
                    ins = eng.activation(out=A_t[pp][:, :],
                                         in_=h_gb[tt][:, pp * 256:(pp + 1) * 256],
                                         func=AF.Square)
                env.inc(ins, s_asl, 1, mark=("p3A", gb, tt))

            def evacs(tt):
                for j in range(2):
                    gi = (gb * NTT + tt) * 2 + j
                    env.wait(eng, s_ptr, ("p3tr", gb * NTT + tt, j))
                    if tt == 0 and j == 0 and gb >= 1:
                        env.wait(eng, s_pac, ("p3d", gb - 1, 31))
                    dst = hsT_gb[:, :].rearrange("p (a n t) -> p a n t", a=8, n=NTT)
                    env.inc(eng.copy(out=dst[:, j * 4:(j + 1) * 4, tt, :],
                                     in_=pT3[gi % 2][:, :].rearrange(
                                         "p (a t) -> p a t", a=4)),
                            s_aev, 1, mark=("p3evT", gb * NTT + tt, j))

            squares(0)
            for tt in range(1, NTT):
                squares(tt)
                evacs(tt - 1)
            evacs(NTT - 1)

    def p3_vector(env, eng):
        for gb in range(NGB):
            for ph in range(4):
                i = gb * 4 + ph
                for tt in range(NTT):
                    gidx = i * NTT + tt
                    env.wait(eng, s_asl, ("p3sl", gidx))
                    if ph == 0 and gb >= 1:
                        env.wait(eng, s_asl, ("p3A", gb - 1, tt))
                    env.inc(eng.tensor_tensor(
                        out=h_gb[tt][:, ph * 256:(ph + 1) * 256],
                        in0=sg_t[gidx % 2][:, :], in1=pGU[gidx % 2][:, 256:512],
                        op=ALU.mult), s_vh, 1, mark=("p3h", gidx))

            def sparsify(tt):
                env.wait(eng, s_asl, ("p3A", gb, tt))
                if gb * NTT + tt >= 2:
                    env.wait(eng, s_ptr, ("p3tr", gb * NTT + tt - 2, 1))
                b01, b02, b03, b12, b13, b23 = [(0, 1), (0, 2), (0, 3),
                                                (1, 2), (1, 3), (2, 3)]

                def cmp(dst, pr):
                    eng.tensor_tensor(out=dst[:, :], in0=A_t[pr[0]][:, :],
                                      in1=A_t[pr[1]][:, :], op=ALU.is_ge)

                def keep(i4, sacc, thr, opk):
                    return eng.scalar_tensor_tensor(
                        out=hs_st[tt % 2][:, i4 * 256:(i4 + 1) * 256],
                        in0=sacc[:, :], scalar=thr,
                        in1=h_gb[tt][:, i4 * 256:(i4 + 1) * 256],
                        op0=opk, op1=ALU.mult)

                wcA, wcB = wc_t
                cmp(wcA, b01)
                cmp(wcB, b02)
                eng.tensor_tensor(out=ss_t[:, :], in0=wcA[:, :], in1=wcB[:, :], op=ALU.add)
                cmp(wcB, b03)
                eng.tensor_tensor(out=ss_t[:, :], in0=ss_t[:, :], in1=wcB[:, :], op=ALU.add)
                keep(0, ss_t, 2.0, ALU.is_ge)
                cmp(wcA, b13)
                eng.tensor_tensor(out=ss_t[:, :], in0=wcB[:, :], in1=wcA[:, :], op=ALU.add)
                cmp(wcB, b23)
                eng.tensor_tensor(out=ss_t[:, :], in0=ss_t[:, :], in1=wcB[:, :], op=ALU.add)
                keep(3, ss_t, 1.0, ALU.is_le)
                cmp(wcB, b12)
                eng.tensor_tensor(out=ss_t[:, :], in0=wcA[:, :], in1=wcB[:, :], op=ALU.add)
                cmp(wcA, b01)
                eng.tensor_tensor(out=ss_t[:, :], in0=ss_t[:, :], in1=wcA[:, :], op=ALU.subtract)
                keep(1, ss_t, 1.0, ALU.is_ge)
                cmp(wcA, b23)
                eng.tensor_tensor(out=ss_t[:, :], in0=wcA[:, :], in1=wcB[:, :], op=ALU.subtract)
                cmp(wcB, b02)
                ins = None
                eng.tensor_tensor(out=ss_t[:, :], in0=ss_t[:, :], in1=wcB[:, :], op=ALU.subtract)
                ins = keep(2, ss_t, 0.0, ALU.is_ge)
                env.inc(ins, s_vhs, 1, mark=("p3hs", gb, tt))

            def down_add(ds, dg, tt):
                gds = gb * 32 + ds
                env.wait(eng, s_pac, ("p3d", gb, ds))
                env.inc(eng.tensor_tensor(
                    out=down_acc[:, tt * D + dg * 512: tt * D + (dg + 1) * 512],
                    in0=down_acc[:, tt * D + dg * 512: tt * D + (dg + 1) * 512],
                    in1=pD[gds % 2][:, :], op=ALU.add),
                    s_vd, 1, mark=("p3dv", gds))

            for tt in range(NTT):
                sparsify(tt)
                if tt >= 1:
                    for dgl in range(2):
                        down_add((tt - 1) * 2 + dgl, dgl, tt - 1)
            for dgl in range(2):
                down_add((NTT - 1) * 2 + dgl, dgl, NTT - 1)
            for dg in range(2, 4):
                for tt in range(NTT):
                    down_add(16 + (dg - 2) * 8 + tt, dg, tt)

    run_phase("p3", {"sync": p3_sync, "gpsimd": p3_gpsimd, "tensor": p3_tensor,
                     "scalar": p3_scalar, "vector": p3_vector})
    ps3.close()
    es3.close()

    def p4_sync(env, eng):
        ov = out_own.rearrange("(n p) d -> p n d", p=128)
        for tt in range(NTT):
            env.wait(eng, s_vd, ("p3dv", (NGB - 1) * 32 + 16 + 8 + tt))
            env.inc(eng.dma_start(out=ov[:, tt, :],
                                  in_=down_acc[:, tt * D:(tt + 1) * D]),
                    s_out, 16, mark=("p4", tt))
        env.wait(eng, s_out, ("p4", NTT - 1))

    run_phase("p4", {"sync": p4_sync})
    esDA.close()
    es.close()
    return nc


def _host_prep(inputs):
    x = np.ascontiguousarray(np.asarray(inputs["L0_attn_x"]).reshape(T, D))
    w = {k: np.ascontiguousarray(np.asarray(v)) for k, v in inputs.items()}
    inv = 10000.0 ** (-np.arange(0, DR, 2, dtype=np.float64) / DR)
    f = np.outer(np.arange(T, dtype=np.float64), inv)
    coskf = np.cos(f).T.astype(np.float32)
    sinkf = np.sin(f).T.astype(np.float32)
    import ml_dtypes
    iota = np.arange(128, dtype=np.float32).reshape(128, 1)
    identb = np.eye(128, dtype=np.float32).astype(ml_dtypes.bfloat16)
    wg = w["L0_ffn_w_gate"]
    wu = w["L0_ffn_w_up"]
    wd = w["L0_ffn_w_down"]
    wg_p = np.ascontiguousarray(wg.reshape(D, FI // 4, 4).transpose(0, 2, 1).reshape(D, FI))
    wu_p = np.ascontiguousarray(wu.reshape(D, FI // 4, 4).transpose(0, 2, 1).reshape(D, FI))
    wd_p = np.ascontiguousarray(wd.reshape(FI // 4, 4, D).transpose(1, 0, 2).reshape(FI, D))

    shared = {
        "x_full": x, "cosk": coskf, "sink": sinkf, "iota": iota, "identb": identb,
        "Wkv": w["L0_attn_Wkv"], "Wkup": w["L0_attn_Wkup"], "Wvup": w["L0_attn_Wvup"],
        "Wq": w["L0_attn_Wq"], "Wqr": w["L0_attn_Wqr"], "Wkr": w["L0_attn_Wkr"],
        "Wo": w["L0_attn_Wo"], "wg_p": wg_p, "wu_p": wu_p, "wd_p": wd_p,
    }
    in_maps = []
    for c in range(NCORES):
        cA, cB = c, 15 - c
        rows = np.r_[cA * 512:(cA + 1) * 512, cB * 512:(cB + 1) * 512]
        m = dict(shared)
        m["x_own"] = np.ascontiguousarray(x[rows])
        m["qpos"] = rows.astype(np.float32).reshape(1, NTOK)
        m["cosq"] = np.ascontiguousarray(coskf[:, rows])
        m["sinq"] = np.ascontiguousarray(sinkf[:, rows])
        wf = np.zeros((1, 16), np.int32)
        wf[0, :(cA + 2) // 2] = 1
        wf[0, 4:4 + (cB + 2) // 2] = 1
        m["win_en"] = wf
        in_maps.append(m)
    return in_maps


def kernel(**inputs):
    in_maps = _host_prep(inputs)
    nc = build_program()
    res = run_bass_kernel_spmd(nc, in_maps, core_ids=list(range(NCORES)))
    out = np.empty((T, D), dtype=np.float32)
    for c in range(NCORES):
        oo = res.results[c]["out_own"]
        cA, cB = c, 15 - c
        out[cA * 512:(cA + 1) * 512] = oo[:512]
        out[cB * 512:(cB + 1) * 512] = oo[512:]
    if _DEBUG:
        kernel._dbg = res.results
    return out.reshape(1, T, D)
